# version 05 (final)
# baseline (speedup 1.0000x reference)
import math
import os
from contextlib import ExitStack
_KSKIP = set(os.environ.get("KSKIP", "").split(","))

import numpy as np
import ml_dtypes
import concourse.bass as bass
import concourse.mybir as mybir
from concourse.bass_utils import run_bass_kernel_spmd

F32 = mybir.dt.float32
BF16 = mybir.dt.bfloat16
AF = mybir.ActivationFunctionType
ALU = mybir.AluOpType
AX = mybir.AxisListType
BB = ml_dtypes.bfloat16

B, T, D = 4, 4096, 2048
FC = 16
HD = 128
H1, KVH1, G1 = 16, 4, 4
KV_DIM = 512
RANK = 16
ROPE1 = 64
E, I_DIM, IC = 4, 2048, 16
TB = 512
NT = 8
OWNT = 4
TOWN = OWNT * TB
C0A = 12.0
C0B = 15.0
NEGB = -1e4
EPS_LN = 1e-5
EPS_RMS = float(np.finfo(np.float32).eps)
SCL = HD ** -0.5
SNAKE = ([0, 3, 4, 7], [1, 2, 5, 6])
PHASE_LIMIT = 99
DEBUG_DUMPS = ()
PARANOID = 0

ENGS = ("tensor", "vector", "scalar", "gpsimd", "sync")
ESEM = {"tensor": "t", "vector": "v", "scalar": "s", "gpsimd": "p", "sync": "y"}
QSEM = {"sync": "q", "gpsimd": "g"}


def _pack_col(W, ocn, fcn):
    return np.ascontiguousarray(
        W.reshape(fcn, 128, ocn, 128).transpose(2, 1, 0, 3).reshape(ocn, 128, fcn * 128))


def _pack_A(A):
    return np.ascontiguousarray(
        A.reshape(FC, 128, RANK).transpose(1, 0, 2).reshape(128, FC * RANK))


def _pack_vec(v):
    return np.ascontiguousarray(v.reshape(FC, 128).T)


def _tri_np():
    out = np.zeros((128, 4 * TB), np.float32)
    q = np.arange(TB)
    for kc in range(4):
        k = kc * 128 + np.arange(128)
        out[:, kc * TB:(kc + 1) * TB] = (k[:, None] <= q[None, :])
    return out.astype(BB)


class _Sem:
    __slots__ = ("h", "name")

    def __init__(self, h, name):
        self.h = h
        self.name = name


class _Op:
    __slots__ = ("eng", "fn", "sem", "idx", "needed", "waits", "amt", "cval",
                 "psrc")

    def __init__(self, eng, fn, sem, idx, amt):
        self.eng = eng
        self.fn = fn
        self.sem = sem
        self.idx = idx
        self.needed = False
        self.waits = []
        self.amt = amt
        self.cval = None
        self.psrc = False


class Buf:
    __slots__ = ("name", "w", "r", "noww", "is_psum")

    def __init__(self, name, noww=False, is_psum=False):
        self.name = name
        self.w = {}
        self.r = {}
        self.noww = noww
        self.is_psum = is_psum


QPOOL = 6


class Prog:
    def __init__(self, nc, ctx):
        self.nc = nc
        self.ctx = ctx
        self.streams = {e: [] for e in ENGS}
        self.sems = {}
        names = ["t", "v", "s", "p", "y", "g", "cc", "bar"] + \
            [f"q{j}" for j in range(QPOOL)]
        for nm in names:
            h = ctx.enter_context(nc.semaphore("sem_" + nm))
            self.sems[nm] = _Sem(h, nm)
        self.counts = {nm: 0 for nm in self.sems}
        self.qrr = 0

    def _mkop(self, eng, fn, semname, amt, always=False):
        sem = self.sems[semname]
        self.counts[semname] += 1
        op = _Op(eng, fn, sem, self.counts[semname], amt)
        if always:
            op.needed = True
        self.streams[eng].append(op)
        return op

    def _deps(self, reads, writes):
        deps = []
        for b in reads:
            deps.extend(b.w.values())
        for b in writes:
            if not b.noww:
                deps.extend(b.w.values())
            deps.extend(b.r.values())
        return deps

    def _track(self, op, reads, writes, semname):
        for b in writes:
            if b.noww:
                b.w[semname] = op
            else:
                b.w = {semname: op}
                b.r = {}
        for b in reads:
            b.r[semname] = op

    def op(self, eng, fn, reads=(), writes=()):
        semname = ESEM[eng]
        deps = self._deps(reads, writes)
        op = self._mkop(eng, fn, semname, 1)
        op.psrc = any(b.is_psum for b in reads)
        for d in deps:
            d.needed = True
            op.waits.append(d)
        self._track(op, reads, writes, semname)
        return op

    def dma(self, eng, out_ap, in_ap, reads=(), writes=()):
        deps = self._deps(reads, writes)
        if eng == "gpsimd":
            semname = "g"
        else:
            semname = f"q{self.qrr % QPOOL}"
            self.qrr += 1
            prev = self.counts[semname]
            if prev:
                m = self._mkop(eng, None, ESEM[eng], 0)
                m.waits = [("ABS", self.sems[semname], prev * 16)]
        op = self._mkop(eng, lambda e, o=out_ap, i=in_ap: e.dma_start(out=o, in_=i),
                        semname, 16, always=True)
        for d in deps:
            d.needed = True
            op.waits.append(d)
        self._track(op, reads, writes, semname)
        if eng == "gpsimd":
            w = self._mkop("gpsimd", None, ESEM["gpsimd"], 0)
            w.waits = [("ABS", self.sems["g"], self.counts["g"] * 16)]
        return op

    def cc(self, fn, reads=(), writes=()):
        deps = self._deps(reads, writes)
        op = self._mkop("gpsimd", fn, "cc", 1, always=True)
        for d in deps:
            d.needed = True
            op.waits.append(d)
        self._track(op, reads, writes, "cc")
        return op

    def barrier(self):
        bs = self.sems["bar"]
        for e in ENGS:
            self._mkop(e, lambda en: en.nop(), "bar", 1, always=True)
        tgt = self.counts["bar"]
        for e in ENGS:
            w = self._mkop(e, None, ESEM[e], 0)
            w.waits = [("ABS", bs, tgt)]
        return tgt

    def finish(self):
        sem_ops = {nm: [] for nm in self.sems}
        for e in ENGS:
            for op in self.streams[e]:
                if op.fn is not None:
                    sem_ops[op.sem.name].append(op)
        for nm, lst in sem_ops.items():
            lst.sort(key=lambda o: o.idx)
            c = 0
            for o in lst:
                if o.needed:
                    c += o.amt
                o.cval = c
        nc = self.nc
        with nc.Block() as block:
            for ename in ENGS:
                deco = getattr(block, ename)

                def body(e, ename=ename):
                    know = {}
                    for op in self.streams[ename]:
                        for d in op.waits:
                            if isinstance(d, tuple):
                                _, sem, val = d
                            else:
                                sem, val = d.sem, d.cval
                            if val and know.get(sem.name, 0) < val:
                                e.wait_ge(sem.h, val)
                                know[sem.name] = val
                        if op.fn is None:
                            continue
                        ins = op.fn(e)
                        if op.needed:
                            ins.then_inc(op.sem.h, op.amt)

                deco(body)


def build_program(phase_limit=None, debug=()):
    if phase_limit is None:
        phase_limit = PHASE_LIMIT
    nc = bass.Bass("TRN2", target_bir_lowering=False, debug=False)
    ctx = ExitStack()
    p = Prog(nc, ctx)

    def din(name, shape, dt=F32):
        return nc.dram_tensor(name, list(shape), dt, kind="ExternalInput").ap()

    def dint(name, shape, dt):
        return nc.dram_tensor(name, list(shape), dt, kind="Internal").ap()

    if "inX" in _KSKIP:
        din0 = din
        din = lambda name, shape, dt=F32: din0(name, [2, 2], dt)
    xfm = din("xfm", [D, T])
    wq1 = din("wq1", [H1, 128, D]); wk1 = din("wk1", [KVH1, 128, D])
    wv1 = din("wv1", [KVH1, 128, D]); wpj = din("wpj", [FC, 128, D])
    wq2 = din("wq2", [FC, 128, D]); wk2 = din("wk2", [FC, 128, D])
    wv2 = din("wv2", [FC, 128, D]); wo2 = din("wo2", [FC, 128, D])
    qa = din("qa", [128, FC * RANK]); qb = din("qb", [RANK, D])
    ka = din("ka", [128, FC * RANK]); kb = din("kb", [RANK, KV_DIM])
    va = din("va", [128, FC * RANK]); vb = din("vb", [RANK, KV_DIM])
    pa = din("pa", [128, FC * RANK]); pb = din("pb", [RANK, D])
    gain = din("gain", [1, H1])
    vecs = {nm: din(nm, [128, FC]) for nm in
            ("bq2", "bk2", "bv2", "bo2", "l1w", "l1b", "l2w", "l2b")}
    wg = din("wg", [128, FC * E])
    we1 = din("we1", [E, IC, 128, D])
    be1 = din("be1", [128, E * IC])
    we2 = din("we2", [E, FC, 128, I_DIM])
    be2 = din("be2", [E, D])
    cos1 = din("cos1", [ROPE1 // 2, T]); sin1 = din("sin1", [ROPE1 // 2, T])
    cos2 = din("cos2", [HD // 2, TOWN]); sin2 = din("sin2", [HD // 2, TOWN])
    biasB1 = din("biasB1", [128, OWNT])
    biasA2 = din("biasA2", [128, OWNT])
    biasB2 = din("biasB2", [128, OWNT])
    mat_in = din("mat_in", [128, 4 * TB], BF16)
    mbt_in = din("mbt_in", [128, 4 * TB], BF16)

    OUT = nc.dram_tensor("out", [D, TOWN], F32, kind="ExternalOutput").ap()

    eye_h = nc.inline_tensor(np.eye(128, dtype=BB), name="eye_h")
    eyef_h = nc.inline_tensor(np.eye(128, dtype=np.float32), name="eyef_h")
    tri_h = nc.inline_tensor(_tri_np(), name="tri_h")
    sel_np = np.zeros((E, E * 128), np.float32)
    for e_ in range(E):
        sel_np[e_, e_ * 128:(e_ + 1) * 128] = 1
    sel_h = nc.inline_tensor(sel_np, name="sel_h")
    CONSTS = sorted({0.0, 1.0, -1.0, 0.5, -0.5, 2.0, -C0A, -C0B, EPS_LN, EPS_RMS,
                     1.0 / D, 1.0 / HD, 1e-30, 1e-12, SCL, NEGB, 1.0 / I_DIM})
    cp_h = nc.inline_tensor(np.tile(np.array([CONSTS], np.float32), (128, 1)),
                            name="cp_h")

    if "dram" in _KSKIP:
        dint0 = dint
        dint = lambda name, shape, dt: dint0(name, [2, 2], dt)
    q1p = dint("q1p", [H1, 128, TOWN], BF16)
    k1p = dint("k1p", [KVH1, 128, T], BF16)
    v1tp = dint("v1tp", [KVH1 * NT * 4 * 128, 129], BF16)
    vnp = dint("vnp", [KVH1, 128, TOWN], F32)
    x1p = dint("x1p", [FC, 128, TOWN], F32)
    q2p = dint("q2p", [H1, 128, TOWN], BF16)
    k2src = dint("k2src", [H1 * 512, TB], BF16)
    k2g = dint("k2g", [2 * H1 * 512, TB], BF16)
    v2src = dint("v2src", [H1 * OWNT * 4 * 128, 129], BF16)
    v2g = dint("v2g", [2 * H1 * OWNT * 4 * 128, 129], BF16)
    hp = dint("hp", [FC, 128, TOWN], F32)
    g2p = dint("g2p", [FC, 128, TOWN], BF16)
    rows_d = dint("rows_d", [4, TOWN], F32)

    DB = {nm: Buf(nm, noww=True) for nm in
          ("q1p", "k1p", "v1tp", "vnp", "x1p", "q2p", "k2src", "k2g",
           "v2src", "v2g", "hp", "g2p", "rows_d", "out")}

    sb = lambda name, shape, dt, c=ctx: c.enter_context(nc.sbuf_tensor(name, list(shape), dt))
    psl = lambda name, shape, dt, c=ctx: c.enter_context(nc.psum_tensor(name, list(shape), dt))

    if "sbufres" in _KSKIP:
        sb0 = sb
        sb = lambda name, shape, dt, c=ctx: sb0(name, [shape[0], 2] if shape[1] > 2 else shape, dt, c)
    cp_sb = sb("cp_sb", [128, len(CONSTS)], F32)
    for i, v in enumerate(CONSTS):
        nc.const_aps.register(F32, float(v), cp_sb[:, i:i + 1])

    ones_rf = sb("ones_rf", [1, 128], F32)
    ones_cb = sb("ones_cb", [128, 1], BF16)
    ones_cf = sb("ones_cf", [128, 1], F32)
    eye_sb = sb("eye_sb", [128, 128], BF16)
    eyef_sb = sb("eyef_sb", [128, 128], F32)
    tri_sb = sb("tri_sb", [128, 4 * TB], BF16)
    gain_sb = sb("gain_sb", [1, H1], F32)
    vec_sb = {nm: sb(nm + "_s", [128, FC], F32) for nm in vecs}
    wg_sb = sb("wg_sb", [128, FC * E], BF16)
    be1_sb = sb("be1_sb", [128, E * IC], F32)
    qa_sb = sb("qa_sb", [128, FC * RANK], BF16); qb_sb = sb("qb_sb", [RANK, D], BF16)
    ka_sb = sb("ka_sb", [128, FC * RANK], BF16); kb_sb = sb("kb_sb", [RANK, KV_DIM], BF16)
    va_sb = sb("va_sb", [128, FC * RANK], BF16); vb_sb = sb("vb_sb", [RANK, KV_DIM], BF16)
    pa_sb = sb("pa_sb", [128, FC * RANK], BF16); pb_sb = sb("pb_sb", [RANK, D], BF16)
    bB1_sb = sb("bB1_sb", [128, OWNT], F32)
    bA2_sb = sb("bA2_sb", [128, OWNT], F32)
    bB2_sb = sb("bB2_sb", [128, OWNT], F32)
    rowscr = sb("rowscr", [1, 8 * TB], F32)
    ropescr = sb("ropescr", [64, 3 * TB], F32)

    if "psum" not in _KSKIP:
        pA = psl("pA", [128, 512], F32)
        pB = psl("pB", [128, 512], F32)
        pLO = psl("pLO", [128, 512], F32)
        pHI = psl("pHI", [128, 512], F32)
        pC = psl("pC", [128, 512], F32)
        pTb = psl("pTb", [128, 128], BF16)
        pTf = psl("pTf", [128, 128], F32)
        pD = psl("pD", [128, 512], F32)
    else:
        pA = pB = pLO = pHI = pC = pTb = pTf = pD = None

    _bufs = {}

    def nb(name):
        b = Buf(name)
        _bufs[name] = b
        return b

    def npb(name):
        b = Buf(name, is_psum=True)
        _bufs[name] = b
        return b

    b_pA, b_pB, b_pLO, b_pHI, b_pC, b_pTb, b_pTf = (
        npb("pA"), npb("pB"), npb("pLO"), npb("pHI"), npb("pC"), npb("pTb"),
        npb("pTf"))
    b_pD0, b_pD1 = npb("pD0"), npb("pD1")
    b_const = Buf("const", noww=True)
    _bufs["const"] = b_const
    b_rowscr = nb("rowscr")
    b_ropescr = nb("ropescr")

    if "sync0" not in _KSKIP:
        for dst, src in ((cp_sb[:], cp_h.ap()), (eye_sb[:], eye_h.ap()),
                         (eyef_sb[:], eyef_h.ap()), (tri_sb[:], tri_h.ap()),
                         (gain_sb[:], gain), (be1_sb[:], be1),
                         (bB1_sb[:], biasB1), (bA2_sb[:], biasA2), (bB2_sb[:], biasB2)):
            p.dma("sync", dst, src, writes=[b_const])
    if "vecs0" not in _KSKIP:
        for nm in vecs:
            p.dma("sync", vec_sb[nm][:], vecs[nm], writes=[b_const])
    if "cast0" not in _KSKIP:
        for dst, src in ((wg_sb[:], wg),
                         (qa_sb[:], qa), (qb_sb[:], qb), (ka_sb[:], ka),
                         (kb_sb[:], kb), (va_sb[:], va), (vb_sb[:], vb),
                         (pa_sb[:], pa), (pb_sb[:], pb)):
            p.dma("gpsimd", dst, src, writes=[b_const])
    if "memset0" not in _KSKIP:
        p.op("vector", lambda e: e.memset(ones_rf[:], 1.0), writes=[b_const])
        p.op("vector", lambda e: e.memset(ones_cb[:], 1.0), writes=[b_const])
        p.op("vector", lambda e: e.memset(ones_cf[:], 1.0), writes=[b_const])
    p.barrier()

    def rope_emit(src_buf, src, dst, dst_buf, cS, sS, half, sign_lo):
        s2 = ropescr[0:half, 0:TB]
        ta = ropescr[0:half, TB:2 * TB]
        tb2 = ropescr[0:half, 2 * TB:3 * TB]
        r = [src_buf, b_const]
        p.op("gpsimd", lambda e: e.tensor_copy(s2, src[half:2 * half, :]), reads=r, writes=[b_ropescr])
        p.op("gpsimd", lambda e: e.tensor_tensor(ta, src[0:half, :], cS, ALU.mult), reads=r, writes=[b_ropescr])
        p.op("gpsimd", lambda e: e.tensor_tensor(tb2, s2, sS, ALU.mult), reads=[b_ropescr, b_const], writes=[b_ropescr])
        if sign_lo > 0:
            p.op("gpsimd", lambda e: e.tensor_tensor(dst[0:half, :], ta, tb2, ALU.add), reads=[b_ropescr], writes=[dst_buf])
        else:
            p.op("gpsimd", lambda e: e.tensor_tensor(dst[0:half, :], ta, tb2, ALU.subtract), reads=[b_ropescr], writes=[dst_buf])
        p.op("gpsimd", lambda e: e.tensor_tensor(ta, src[0:half, :], sS, ALU.mult), reads=r, writes=[b_ropescr])
        p.op("gpsimd", lambda e: e.tensor_tensor(tb2, s2, cS, ALU.mult), reads=[b_ropescr, b_const], writes=[b_ropescr])
        if sign_lo > 0:
            p.op("gpsimd", lambda e: e.tensor_tensor(dst[half:2 * half, :], tb2, ta, ALU.subtract), reads=[b_ropescr], writes=[dst_buf])
        else:
            p.op("gpsimd", lambda e: e.tensor_tensor(dst[half:2 * half, :], tb2, ta, ALU.add), reads=[b_ropescr], writes=[dst_buf])
        if 2 * half < 128:
            p.op("gpsimd", lambda e: e.tensor_copy(dst[2 * half:128, :], src[2 * half:128, :]),
                 reads=r, writes=[dst_buf])

    ctx1 = ExitStack()
    cos1_sb = sb("cos1_sb", [32, T], F32, ctx1)
    sin1_sb = sb("sin1_sb", [32, T], F32, ctx1)
    if "cosdma" not in _KSKIP:
        p.dma("sync", cos1_sb[:], cos1, writes=[b_const])
        p.dma("sync", sin1_sb[:], sin1, writes=[b_const])
    xt = sb("xt", [128, FC * TB], F32, ctx1)
    b_xt = [nb(f"xt{fc}") for fc in range(FC)]
    xb2 = sb("xb2", [128, 2 * TB], BF16, ctx1)
    sq2 = sb("sq2", [128, 2 * TB], BF16, ctx1)
    b_xb = [nb("xb0"), nb("xb1")]
    b_sq = [nb("sq0"), nb("sq1")]
    lnx = sb("lnx", [128, FC * TB], BF16, ctx1)
    b_lnx = nb("lnx")
    kvw_sb = sb("kvw_sb", [128, 8 * D], BF16, ctx1)
    b_kvw = nb("kvw")
    wslab = sb("wslab", [128, 2 * D], BF16, ctx1)
    b_wslab = [nb("wsl0"), nb("wsl1")]
    t1k = sb("t1k", [RANK, TB], BF16, ctx1); b_t1k = nb("t1k")
    t1v = sb("t1v", [RANK, TB], BF16, ctx1); b_t1v = nb("t1v")
    t1q = sb("t1q", [RANK, TB], BF16, ctx1); b_t1q = nb("t1q")
    kf = sb("kf", [128, TB], F32, ctx1); b_kf = nb("kf")
    lntmp = sb("lntmp", [128, TB], F32, ctx1); b_lntmp = nb("lntmp")
    sqk = sb("sqk", [128, TB], BF16, ctx1); b_sqk = nb("sqk")
    kd = sb("kd", [128, TB], BF16, ctx1); b_kd = nb("kd")
    vstage = sb("vstage", [128, 4 * 129], BF16, ctx1); b_vstage = nb("vstage")
    vnsc = sb("vnsc", [128, TB], F32, ctx1); b_vnsc = nb("vnsc")

    R_MU = rowscr[0:1, 0:TB]
    R_T = rowscr[0:1, TB:2 * TB]
    R_VAR = rowscr[0:1, 2 * TB:3 * TB]
    R_RS = rowscr[0:1, 3 * TB:4 * TB]
    R_MR = rowscr[0:1, 4 * TB:5 * TB]
    R_L = rowscr[0:1, 5 * TB:6 * TB]
    R_R = rowscr[0:1, 6 * TB:7 * TB]
    R_X = rowscr[0:1, 7 * TB:8 * TB]

    def rms_rows(eps, scale_, exp_scale):
        p.op("scalar", lambda e: e.activation(R_L, pD[0:1, :], AF.Ln, bias=eps, scale=scale_),
             reads=[b_pD0], writes=[b_rowscr])
        p.op("scalar", lambda e, es=exp_scale: e.activation(R_R, R_L, AF.Exp, bias=0.0, scale=es),
             reads=[b_rowscr], writes=[b_rowscr])

    def proj_epilogue_qk(oc, tcols, dst_dram, dram_buf, gain_col):
        p.op("vector", lambda e: e.tensor_copy(kf[:], pA[:]), reads=[b_pA], writes=[b_kf])
        p.op("vector", lambda e: e.tensor_tensor(sqk[:], kf[:], kf[:], ALU.mult),
             reads=[b_kf], writes=[b_sqk])
        p.op("tensor", lambda e: e.matmul(pD[0:1, :], ones_cb[:], sqk[:], start=True, stop=True),
             reads=[b_sqk, b_const], writes=[b_pD0])
        rms_rows(EPS_RMS, 1.0 / HD, -0.5)
        if gain_col is not None:
            p.op("vector", lambda e, g=gain_col: e.tensor_scalar(R_R, R_R, g, None, ALU.mult),
                 reads=[b_rowscr, b_const], writes=[b_rowscr])
        p.op("tensor", lambda e: e.matmul(pB[:], ones_rf[:], R_R, start=True, stop=True),
             reads=[b_rowscr, b_const], writes=[b_pB])
        p.op("vector", lambda e: e.tensor_tensor(kf[:], kf[:], pB[:], ALU.mult),
             reads=[b_pB], writes=[b_kf])
        rope_emit(b_kf, kf, kd, b_kd, cos1_sb[:, tcols], sin1_sb[:, tcols], 32, +1)
        p.dma("sync", dst_dram, kd[:], reads=[b_kd], writes=[dram_buf])

    def emit_p1():
        for oc in range(KVH1):
            p.dma("gpsimd", kvw_sb[:, oc * D:(oc + 1) * D], wk1[oc], writes=[b_kvw])
            p.dma("gpsimd", kvw_sb[:, (KVH1 + oc) * D:(KVH1 + oc + 1) * D], wv1[oc],
                  writes=[b_kvw])
        p.op("vector", lambda e: e.memset(vstage[:], 1.0), writes=[b_vstage])
        xr = xfm.rearrange("(fc p) T -> p fc T", p=128)
        for t in range(NT):
            tcols = slice(t * TB, (t + 1) * TB)
            for fc in range(FC):
                p.dma("sync", xt[:, fc * TB:(fc + 1) * TB],
                      xr[:, fc, tcols], writes=[b_xt[fc]])
            for fc in range(FC):
                sl = fc % 2
                xbv = xb2[:, sl * TB:(sl + 1) * TB]
                sqv = sq2[:, sl * TB:(sl + 1) * TB]
                xtv = xt[:, fc * TB:(fc + 1) * TB]
                p.op("vector", lambda e, o=xbv, i=xtv: e.tensor_copy(o, i),
                     reads=[b_xt[fc]], writes=[b_xb[sl]])
                p.op("vector", lambda e, o=sqv, i=xbv: e.tensor_tensor(o, i, i, ALU.mult),
                     reads=[b_xb[sl]], writes=[b_sq[sl]])
                p.op("tensor", lambda e, r=xbv, s=(fc == 0), st=(fc == FC - 1):
                     e.matmul(pD[0:1, :], ones_cb[:], r, start=s, stop=st),
                     reads=[b_xb[sl], b_const], writes=[b_pD0])
                p.op("tensor", lambda e, r=sqv, s=(fc == 0), st=(fc == FC - 1):
                     e.matmul(pD[32:33, :], ones_cb[:], r, start=s, stop=st),
                     reads=[b_sq[sl], b_const], writes=[b_pD1])
            p.op("vector", lambda e: e.tensor_scalar(R_MU, pD[0:1, :], 1.0 / D, None, ALU.mult),
                 reads=[b_pD0], writes=[b_rowscr])
            p.op("vector", lambda e: e.tensor_scalar(R_T, pD[32:33, :], 1.0 / D, None, ALU.mult),
                 reads=[b_pD1], writes=[b_rowscr])
            p.op("vector", lambda e: e.tensor_tensor(R_VAR, R_MU, R_MU, ALU.mult),
                 reads=[b_rowscr], writes=[b_rowscr])
            p.op("vector", lambda e: e.tensor_tensor(R_VAR, R_T, R_VAR, ALU.subtract),
                 reads=[b_rowscr], writes=[b_rowscr])
            p.op("scalar", lambda e: e.activation(R_T, R_VAR, AF.Ln, bias=EPS_LN, scale=1.0),
                 reads=[b_rowscr], writes=[b_rowscr])
            p.op("scalar", lambda e: e.activation(R_RS, R_T, AF.Exp, bias=0.0, scale=-0.5),
                 reads=[b_rowscr], writes=[b_rowscr])
            p.op("vector", lambda e: e.tensor_tensor(R_MR, R_MU, R_RS, ALU.mult),
                 reads=[b_rowscr], writes=[b_rowscr])
            p.op("tensor", lambda e: e.matmul(pA[:], ones_rf[:], R_RS, start=True, stop=True),
                 reads=[b_rowscr, b_const], writes=[b_pA])
            p.op("tensor", lambda e: e.matmul(pB[:], ones_rf[:], R_MR, start=True, stop=True),
                 reads=[b_rowscr, b_const], writes=[b_pB])
            for fc in range(FC):
                xtv = xt[:, fc * TB:(fc + 1) * TB]
                p.op("vector", lambda e, i=xtv: e.tensor_tensor(lntmp[:], i, pA[:], ALU.mult),
                     reads=[b_xt[fc], b_pA], writes=[b_lntmp])
                p.op("vector", lambda e, o=lnx[:, fc * TB:(fc + 1) * TB]:
                     e.tensor_tensor(o, lntmp[:], pB[:], ALU.subtract),
                     reads=[b_lntmp, b_pB], writes=[b_lnx])
            lr = [(ka_sb, t1k, b_t1k), (va_sb, t1v, b_t1v)]
            if t < OWNT:
                lr.append((qa_sb, t1q, b_t1q))
            for A_s, t1b, t1bf in lr:
                for fc in range(FC):
                    p.op("tensor", lambda e, l=A_s[:, fc * RANK:(fc + 1) * RANK],
                         r=lnx[:, fc * TB:(fc + 1) * TB],
                         s=(fc == 0), st=(fc == FC - 1):
                         e.matmul(pC[0:RANK, :], l, r, start=s, stop=st),
                         reads=[b_lnx, b_const], writes=[b_pC])
                p.op("vector", lambda e, o=t1b[:]: e.tensor_copy(o, pC[0:RANK, :]),
                     reads=[b_pC], writes=[t1bf])
            for kind in ("k", "v"):
                for oc in range(KVH1):
                    wsl = oc if kind == "k" else KVH1 + oc
                    Bsb = kb_sb if kind == "k" else vb_sb
                    t1b, t1bf = (t1k, b_t1k) if kind == "k" else (t1v, b_t1v)
                    for fc in range(FC):
                        p.op("tensor", lambda e,
                             l=kvw_sb[:, wsl * D + fc * 128: wsl * D + (fc + 1) * 128],
                             r=lnx[:, fc * TB:(fc + 1) * TB], s=(fc == 0):
                             e.matmul(pA[:], l, r, start=s, stop=False),
                             reads=[b_lnx, b_kvw], writes=[b_pA])
                    p.op("tensor", lambda e, l=Bsb[:, oc * 128:(oc + 1) * 128], r=t1b[:]:
                         e.matmul(pA[:], l, r, start=False, stop=True),
                         reads=[t1bf, b_const], writes=[b_pA])
                    if kind == "k":
                        proj_epilogue_qk(oc, tcols, k1p[oc, :, tcols], DB["k1p"], None)
                    else:
                        p.op("vector", lambda e: e.tensor_copy(kf[:], pA[:]),
                             reads=[b_pA], writes=[b_kf])
                        if t < OWNT:
                            p.op("vector", lambda e: e.tensor_tensor(sqk[:], kf[:], kf[:], ALU.mult),
                                 reads=[b_kf], writes=[b_sqk])
                            p.op("tensor", lambda e: e.matmul(pD[0:1, :], ones_cb[:], sqk[:],
                                                              start=True, stop=True),
                                 reads=[b_sqk, b_const], writes=[b_pD0])
                            rms_rows(1e-30, 1.0, 0.5)
                            p.op("vector", lambda e: e.tensor_scalar(R_R, R_R, 1e-12, None, ALU.max),
                                 reads=[b_rowscr], writes=[b_rowscr])
                            p.op("vector", lambda e: e.reciprocal(R_R, R_R),
                                 reads=[b_rowscr], writes=[b_rowscr])
                            p.op("tensor", lambda e: e.matmul(pB[:], ones_rf[:], R_R, start=True, stop=True),
                                 reads=[b_rowscr, b_const], writes=[b_pB])
                            p.op("vector", lambda e: e.tensor_tensor(vnsc[:], kf[:], pB[:], ALU.mult),
                                 reads=[b_kf, b_pB], writes=[b_vnsc])
                            p.dma("sync", vnp[oc, :, tcols], vnsc[:],
                                  reads=[b_vnsc], writes=[DB["vnp"]])
                        p.op("vector", lambda e: e.tensor_copy(kd[:], kf[:]),
                             reads=[b_kf], writes=[b_kd])
                        for kc in range(4):
                            p.op("tensor", lambda e, i=kd[:, kc * 128:(kc + 1) * 128]:
                                 e.transpose(pTb[:], i, eye_sb[:]),
                                 reads=[b_kd, b_const], writes=[b_pTb])
                            p.op("vector", lambda e, o=vstage[:, kc * 129:kc * 129 + 128]:
                                 e.tensor_copy(o, pTb[:]),
                                 reads=[b_pTb], writes=[b_vstage])
                        base = (oc * NT + t) * 512
                        p.dma("sync",
                              v1tp[base:base + 512, :].rearrange("(kc p) c -> p kc c", p=128),
                              vstage[:].rearrange("p (kc c) -> p kc c", kc=4),
                              reads=[b_vstage], writes=[DB["v1tp"]])
            if t < OWNT:
                for oc in range(H1):
                    wsl = oc % 2
                    p.dma("gpsimd", wslab[:, wsl * D:(wsl + 1) * D], wq1[oc],
                          writes=[b_wslab[wsl]])
                    for fc in range(FC):
                        p.op("tensor", lambda e,
                             l=wslab[:, wsl * D + fc * 128: wsl * D + (fc + 1) * 128],
                             r=lnx[:, fc * TB:(fc + 1) * TB], s=(fc == 0):
                             e.matmul(pA[:], l, r, start=s, stop=False),
                             reads=[b_lnx, b_wslab[wsl]], writes=[b_pA])
                    p.op("tensor", lambda e, l=qb_sb[:, oc * 128:(oc + 1) * 128], r=t1q[:]:
                         e.matmul(pA[:], l, r, start=False, stop=True),
                         reads=[b_t1q, b_const], writes=[b_pA])
                    proj_epilogue_qk(oc, tcols, q1p[oc, :, tcols], DB["q1p"],
                                     gain_sb[0:1, oc:oc + 1])

    if phase_limit >= 1:
        emit_p1()
    p.barrier()
    if "noclose" not in _KSKIP:
        ctx1.close()

    def emit_p23():
        ctx2 = ExitStack()
        Kb = sb("Kb", [128, 8 * TB], BF16, ctx2)
        b_Ks = [nb(f"Ks{s}") for s in range(8)]
        Vtb = sb("Vtb", [128, 8 * 4 * 129], BF16, ctx2)
        b_Vs = [nb(f"Vs{s}") for s in range(8)]
        vns = sb("vns", [128, TB], F32, ctx2); b_vns = nb("vns2")
        qs2 = sb("qs2", [128, 2 * TB], BF16, ctx2)
        b_qs = [nb("qsl0"), nb("qsl1")]
        Pb = sb("Pb", [128, 3 * TB], BF16, ctx2)
        b_P = [nb(f"P{j}") for j in range(3)]
        ysb = sb("ysb", [128, TB], F32, ctx2); b_ysb = nb("ysb")
        tt2 = sb("tt2", [128, TB], F32, ctx2); b_tt2 = nb("tt2")
        y_acc = sb("y_acc", [128, H1 * TOWN], BF16, ctx2); b_yacc = nb("yacc")
        xr2 = xfm.rearrange("(fc p) T -> p fc T", p=128)
        pcnt = [0]
        for kvh in range(KVH1):
            for i in range(OWNT):
                for pos in (i, 4 + i):
                    p.dma("sync", Kb[:, pos * TB:(pos + 1) * TB],
                          k1p[kvh, :, pos * TB:(pos + 1) * TB],
                          reads=[DB["k1p"]], writes=[b_Ks[pos]])
                    vb_ = (kvh * NT + pos) * 512
                    p.dma("sync",
                          Vtb[:, pos * 516:(pos + 1) * 516].rearrange("p (kc c) -> p kc c", kc=4),
                          v1tp[vb_:vb_ + 512, :].rearrange("(kc p) c -> p kc c", p=128),
                          reads=[DB["v1tp"]], writes=[b_Vs[pos]])
                p.dma("sync", vns[:], vnp[kvh, :, i * TB:(i + 1) * TB],
                      reads=[DB["vnp"]], writes=[b_vns])
                POS = list(range(0, i + 1)) + list(range(4, 5 + i))
                for g in range(G1):
                    h = kvh * G1 + g
                    qsl = h % 2
                    qv = qs2[:, qsl * TB:(qsl + 1) * TB]
                    p.dma("sync", qv, q1p[h, :, i * TB:(i + 1) * TB],
                          reads=[DB["q1p"]], writes=[b_qs[qsl]])
                    first = True
                    for idx, pos in enumerate(POS):
                        for kc in range(4):
                            sp_, b_sp = ((pA, b_pA) if (idx * 4 + kc) % 2 == 0
                                         else (pB, b_pB))
                            p.op("tensor", lambda e, o=sp_,
                                 l=Kb[:, pos * TB + kc * 128: pos * TB + (kc + 1) * 128],
                                 r=qv: e.matmul(o[:], l, r, start=True, stop=True),
                                 reads=[b_Ks[pos], b_qs[qsl]], writes=[b_sp])
                            pj = pcnt[0] % 3
                            pcnt[0] += 1
                            Pv = Pb[:, pj * TB:(pj + 1) * TB]
                            bias = bB1_sb[:, i:i + 1] if pos == 4 + i else -C0A
                            p.op("scalar", lambda e, o=Pv, ii=sp_, bi=bias:
                                 e.activation(o, ii[:], AF.Exp, bias=bi, scale=SCL),
                                 reads=[b_sp, b_const], writes=[b_P[pj]])
                            if pos == i:
                                p.op("vector", lambda e, o=Pv,
                                     m=tri_sb[:, kc * TB:(kc + 1) * TB]:
                                     e.tensor_tensor(o, o, m, ALU.mult),
                                     reads=[b_P[pj], b_const], writes=[b_P[pj]])
                            last = (idx == len(POS) - 1 and kc == 3)
                            vb2 = pos * 516 + kc * 129
                            p.op("tensor", lambda e, l=Vtb[:, vb2:vb2 + 65], r=Pv,
                                 s=first, st=last:
                                 e.matmul(pLO[0:65, :], l, r, start=s, stop=st),
                                 reads=[b_Vs[pos], b_P[pj]], writes=[b_pLO])
                            p.op("tensor", lambda e, l=Vtb[:, vb2 + 64:vb2 + 129], r=Pv,
                                 s=first, st=last:
                                 e.matmul(pHI[0:65, :], l, r, start=s, stop=st),
                                 reads=[b_Vs[pos], b_P[pj]], writes=[b_pHI])
                            first = False
                    p.op("vector", lambda e: e.tensor_copy(ysb[0:64, :], pLO[0:64, :]),
                         reads=[b_pLO], writes=[b_ysb])
                    p.op("vector", lambda e: e.tensor_copy(ysb[64:128, :], pHI[0:64, :]),
                         reads=[b_pHI], writes=[b_ysb])
                    p.op("vector", lambda e: e.tensor_copy(R_X, pHI[64:65, :]),
                         reads=[b_pHI], writes=[b_rowscr])
                    p.op("vector", lambda e: e.reciprocal(R_X, R_X),
                         reads=[b_rowscr], writes=[b_rowscr])
                    p.op("tensor", lambda e: e.matmul(pC[:], ones_rf[:], R_X,
                                                      start=True, stop=True),
                         reads=[b_rowscr, b_const], writes=[b_pC])
                    p.op("vector", lambda e: e.tensor_tensor(tt2[:], ysb[:], vns[:], ALU.mult),
                         reads=[b_ysb, b_vns], writes=[b_tt2])
                    p.op("tensor", lambda e: e.matmul(pD[0:1, :], ones_cf[:], tt2[:],
                                                      start=True, stop=True),
                         reads=[b_tt2, b_const], writes=[b_pD0])
                    p.op("vector", lambda e: e.tensor_copy(R_L, pD[0:1, :]),
                         reads=[b_pD0], writes=[b_rowscr])
                    p.op("tensor", lambda e: e.matmul(pA[:], ones_rf[:], R_L,
                                                      start=True, stop=True),
                         reads=[b_rowscr, b_const], writes=[b_pA])
                    p.op("vector", lambda e: e.tensor_tensor(tt2[:], vns[:], pA[:], ALU.mult),
                         reads=[b_vns, b_pA], writes=[b_tt2])
                    p.op("vector", lambda e: e.tensor_tensor(ysb[:], ysb[:], tt2[:], ALU.subtract),
                         reads=[b_tt2], writes=[b_ysb])
                    p.op("vector", lambda e,
                         o=y_acc[:, h * TOWN + i * TB: h * TOWN + (i + 1) * TB]:
                         e.tensor_tensor(o, ysb[:], pC[:], ALU.mult),
                         reads=[b_ysb, b_pC], writes=[b_yacc])
                    if PARANOID:
                        p.barrier()
        if "y1" in debug:
            o = nc.dram_tensor("dbg_y1", [128, H1 * TOWN], BF16,
                               kind="ExternalOutput").ap()
            p.dma("sync", o, y_acc[:], reads=[b_yacc], writes=[Buf("dbgy1")])
        t1p = sb("t1p", [RANK, OWNT * TB], BF16, ctx2); b_t1p = nb("t1p")
        statrow = sb("statrow", [1, 8 * TB], F32, ctx2); b_stat = nb("statrow")
        wsl2 = sb("wsl2", [128, 2 * D], BF16, ctx2)
        b_wsl2 = [nb("w2a"), nb("w2b")]
        xc2 = sb("xc2", [128, 2 * TB], F32, ctx2)
        b_xc2 = [nb("xca"), nb("xcb")]
        x1sb = sb("x1sb", [128, TB], F32, ctx2); b_x1sb = nb("x1sb")
        x1b6 = sb("x1b6", [128, TB], BF16, ctx2); b_x1b6 = nb("x1b6")
        sq6 = sb("sq6", [128, TB], BF16, ctx2); b_sq6 = nb("sq6")
        for i in range(OWNT):
            for fc in range(FC):
                p.op("tensor", lambda e, l=pa_sb[:, fc * RANK:(fc + 1) * RANK],
                     r=y_acc[:, fc * TOWN + i * TB: fc * TOWN + (i + 1) * TB],
                     s=(fc == 0), st=(fc == FC - 1):
                     e.matmul(pC[0:RANK, :], l, r, start=s, stop=st),
                     reads=[b_yacc, b_const], writes=[b_pC])
            p.op("vector", lambda e, o=t1p[:, i * TB:(i + 1) * TB]:
                 e.tensor_copy(o, pC[0:RANK, :]), reads=[b_pC], writes=[b_t1p])
        for oc in range(FC):
            wsl = oc % 2
            p.dma("gpsimd", wsl2[:, wsl * D:(wsl + 1) * D], wpj[oc],
                  writes=[b_wsl2[wsl]])
            for i in range(OWNT):
                for fc in range(FC):
                    p.op("tensor", lambda e,
                         l=wsl2[:, wsl * D + fc * 128: wsl * D + (fc + 1) * 128],
                         r=y_acc[:, fc * TOWN + i * TB: fc * TOWN + (i + 1) * TB],
                         s=(fc == 0): e.matmul(pA[:], l, r, start=s, stop=False),
                         reads=[b_yacc, b_wsl2[wsl]], writes=[b_pA])
                p.op("tensor", lambda e, l=pb_sb[:, oc * 128:(oc + 1) * 128],
                     r=t1p[:, i * TB:(i + 1) * TB]:
                     e.matmul(pA[:], l, r, start=False, stop=True),
                     reads=[b_t1p, b_const], writes=[b_pA])
                xsl = i % 2
                xv = xc2[:, xsl * TB:(xsl + 1) * TB]
                p.dma("sync", xv, xr2[:, oc, i * TB:(i + 1) * TB], writes=[b_xc2[xsl]])
                p.op("vector", lambda e, x_=xv: e.tensor_tensor(x1sb[:], pA[:], x_, ALU.add),
                     reads=[b_pA, b_xc2[xsl]], writes=[b_x1sb])
                p.dma("sync", x1p[oc, :, i * TB:(i + 1) * TB], x1sb[:],
                      reads=[b_x1sb], writes=[DB["x1p"]])
                p.op("vector", lambda e: e.tensor_copy(x1b6[:], x1sb[:]),
                     reads=[b_x1sb], writes=[b_x1b6])
                p.op("vector", lambda e: e.tensor_tensor(sq6[:], x1b6[:], x1b6[:], ALU.mult),
                     reads=[b_x1b6], writes=[b_sq6])
                p.op("tensor", lambda e: e.matmul(pD[0:1, :], ones_cb[:], x1b6[:],
                                                  start=True, stop=True),
                     reads=[b_x1b6, b_const], writes=[b_pD0])
                p.op("tensor", lambda e: e.matmul(pD[32:33, :], ones_cb[:], sq6[:],
                                                  start=True, stop=True),
                     reads=[b_sq6, b_const], writes=[b_pD1])
                s1v = statrow[0:1, i * TB:(i + 1) * TB]
                s2v = statrow[0:1, (4 + i) * TB:(5 + i) * TB]
                if oc == 0:
                    p.op("vector", lambda e, o=s1v: e.tensor_copy(o, pD[0:1, :]),
                         reads=[b_pD0], writes=[b_stat])
                    p.op("vector", lambda e, o=s2v: e.tensor_copy(o, pD[32:33, :]),
                         reads=[b_pD1], writes=[b_stat])
                else:
                    p.op("vector", lambda e, o=s1v: e.tensor_tensor(o, o, pD[0:1, :], ALU.add),
                         reads=[b_pD0], writes=[b_stat])
                    p.op("vector", lambda e, o=s2v: e.tensor_tensor(o, o, pD[32:33, :], ALU.add),
                         reads=[b_pD1], writes=[b_stat])
                if PARANOID:
                    p.barrier()
        for i in range(OWNT):
            s1v = statrow[0:1, i * TB:(i + 1) * TB]
            s2v = statrow[0:1, (4 + i) * TB:(5 + i) * TB]
            p.op("vector", lambda e, s=s1v: e.tensor_scalar(R_MU, s, 1.0 / D, None, ALU.mult),
                 reads=[b_stat], writes=[b_rowscr])
            p.op("vector", lambda e, s=s2v: e.tensor_scalar(R_T, s, 1.0 / D, None, ALU.mult),
                 reads=[b_stat], writes=[b_rowscr])
            p.op("vector", lambda e: e.tensor_tensor(R_VAR, R_MU, R_MU, ALU.mult),
                 reads=[b_rowscr], writes=[b_rowscr])
            p.op("vector", lambda e: e.tensor_tensor(R_VAR, R_T, R_VAR, ALU.subtract),
                 reads=[b_rowscr], writes=[b_rowscr])
            p.op("scalar", lambda e: e.activation(R_L, R_VAR, AF.Ln, bias=EPS_LN, scale=1.0),
                 reads=[b_rowscr], writes=[b_rowscr])
            p.op("scalar", lambda e: e.activation(R_RS, R_L, AF.Exp, bias=0.0, scale=-0.5),
                 reads=[b_rowscr], writes=[b_rowscr])
            p.dma("sync", rows_d[0:1, i * TB:(i + 1) * TB], R_MU,
                  reads=[b_rowscr], writes=[DB["rows_d"]])
            p.dma("sync", rows_d[1:2, i * TB:(i + 1) * TB], R_RS,
                  reads=[b_rowscr], writes=[DB["rows_d"]])
            p.op("vector", lambda e: e.tensor_scalar(R_T, R_VAR, EPS_LN, None, ALU.add),
                 reads=[b_rowscr], writes=[b_rowscr])
            p.op("vector", lambda e: e.reciprocal(R_T, R_T), reads=[b_rowscr], writes=[b_rowscr])
            p.op("vector", lambda e: e.tensor_tensor(R_T, R_VAR, R_T, ALU.mult),
                 reads=[b_rowscr], writes=[b_rowscr])
            p.op("scalar", lambda e: e.activation(R_T, R_T, AF.Ln, bias=EPS_LN, scale=1.0),
                 reads=[b_rowscr], writes=[b_rowscr])
            p.op("vector", lambda e: e.tensor_tensor(R_T, R_T, R_L, ALU.add),
                 reads=[b_rowscr], writes=[b_rowscr])
            p.op("scalar", lambda e: e.activation(R_X, R_T, AF.Exp, bias=0.0, scale=-0.5),
                 reads=[b_rowscr], writes=[b_rowscr])
            p.op("vector", lambda e: e.tensor_tensor(R_MR, R_MU, R_X, ALU.mult),
                 reads=[b_rowscr], writes=[b_rowscr])
            p.dma("sync", rows_d[2:3, i * TB:(i + 1) * TB], R_X,
                  reads=[b_rowscr], writes=[DB["rows_d"]])
            p.dma("sync", rows_d[3:4, i * TB:(i + 1) * TB], R_MR,
                  reads=[b_rowscr], writes=[DB["rows_d"]])
        ctx2.close()
        p.barrier()
        ctx3 = ExitStack()
        g1_sb = sb("g1_sb", [128, FC * TOWN], BF16, ctx3)
        b_g1 = nb("g1")
        xg = sb("xg", [128, 2 * TB], F32, ctx3)
        b_xg = [nb("xga"), nb("xgb")]
        gt = sb("gt", [128, TB], F32, ctx3); b_gt = nb("gt")
        for i in range(OWNT):
            p.dma("sync", R_L, rows_d[2:3, i * TB:(i + 1) * TB],
                  reads=[DB["rows_d"]], writes=[b_rowscr])
            p.dma("sync", R_MR, rows_d[3:4, i * TB:(i + 1) * TB],
                  reads=[DB["rows_d"]], writes=[b_rowscr])
            p.op("tensor", lambda e: e.matmul(pA[:], ones_rf[:], R_L, start=True, stop=True),
                 reads=[b_rowscr, b_const], writes=[b_pA])
            p.op("tensor", lambda e: e.matmul(pB[:], ones_rf[:], R_MR, start=True, stop=True),
                 reads=[b_rowscr, b_const], writes=[b_pB])
            for fc in range(FC):
                xsl = fc % 2
                xv = xg[:, xsl * TB:(xsl + 1) * TB]
                p.dma("sync", xv, x1p[fc, :, i * TB:(i + 1) * TB],
                      reads=[DB["x1p"]], writes=[b_xg[xsl]])
                p.op("vector", lambda e, x_=xv: e.tensor_tensor(gt[:], x_, pA[:], ALU.mult),
                     reads=[b_xg[xsl], b_pA], writes=[b_gt])
                p.op("vector", lambda e: e.tensor_tensor(gt[:], gt[:], pB[:], ALU.subtract),
                     reads=[b_pB], writes=[b_gt])
                p.op("vector", lambda e,
                     o=g1_sb[:, fc * TOWN + i * TB: fc * TOWN + (i + 1) * TB],
                     w1=vec_sb["l1w"][:, fc:fc + 1], b1=vec_sb["l1b"][:, fc:fc + 1]:
                     e.tensor_scalar(o, gt[:], w1, b1, ALU.mult, ALU.add),
                     reads=[b_gt, b_const], writes=[b_g1])
                if PARANOID:
                    p.barrier()
        if "g1" in debug:
            o = nc.dram_tensor("dbg_g1", [128, FC * TOWN], BF16,
                               kind="ExternalOutput").ap()
            p.dma("sync", o, g1_sb[:], reads=[b_g1], writes=[Buf("dbgg1")])
        return ctx3, g1_sb, b_g1

    def emit_p3b(g1_sb, b_g1):
        c3 = ExitStack()
        cos2_sb = sb("cos2_sb", [64, TOWN], F32, c3)
        sin2_sb = sb("sin2_sb", [64, TOWN], F32, c3)
        b_tab2 = Buf("tab2", noww=True)
        p.dma("sync", cos2_sb[:], cos2, writes=[b_tab2])
        p.dma("sync", sin2_sb[:], sin2, writes=[b_tab2])
        wsl3 = sb("wsl3", [128, 2 * D], BF16, c3)
        b_wsl3 = [nb("w3a"), nb("w3b")]
        pf3 = sb("pf3", [128, TB], F32, c3); b_pf3 = nb("pf3")
        pd3 = sb("pd3", [128, TB], BF16, c3); b_pd3 = nb("pd3")
        vst2 = sb("vst2", [128, 4 * 129], BF16, c3); b_vst2 = nb("vst2")
        p.op("vector", lambda e: e.memset(vst2[:], 1.0), writes=[b_vst2])

        def rope2(tcols):
            cS = cos2_sb[:, tcols]
            sS = sin2_sb[:, tcols]
            s2 = ropescr[0:64, 0:TB]
            ta = ropescr[0:64, TB:2 * TB]
            tb3 = ropescr[0:64, 2 * TB:3 * TB]
            r = [b_pf3, b_tab2]
            p.op("gpsimd", lambda e: e.tensor_copy(s2, pf3[64:128, :]), reads=r, writes=[b_ropescr])
            p.op("gpsimd", lambda e: e.tensor_tensor(ta, pf3[0:64, :], cS, ALU.mult), reads=r, writes=[b_ropescr])
            p.op("gpsimd", lambda e: e.tensor_tensor(tb3, s2, sS, ALU.mult), reads=[b_ropescr, b_tab2], writes=[b_ropescr])
            p.op("gpsimd", lambda e: e.tensor_tensor(pd3[0:64, :], ta, tb3, ALU.subtract), reads=[b_ropescr], writes=[b_pd3])
            p.op("gpsimd", lambda e: e.tensor_tensor(ta, pf3[0:64, :], sS, ALU.mult), reads=r, writes=[b_ropescr])
            p.op("gpsimd", lambda e: e.tensor_tensor(tb3, s2, cS, ALU.mult), reads=[b_ropescr, b_tab2], writes=[b_ropescr])
            p.op("gpsimd", lambda e: e.tensor_tensor(pd3[64:128, :], tb3, ta, ALU.add), reads=[b_ropescr], writes=[b_pd3])

        for kind, wsrc, bnm in (("k", wk2, "bk2"), ("v", wv2, "bv2"), ("q", wq2, "bq2")):
            for oc in range(FC):
                wsl = oc % 2
                p.dma("gpsimd", wsl3[:, wsl * D:(wsl + 1) * D], wsrc[oc],
                      writes=[b_wsl3[wsl]])
                for i in range(OWNT):
                    for fc in range(FC):
                        p.op("tensor", lambda e,
                             l=wsl3[:, wsl * D + fc * 128: wsl * D + (fc + 1) * 128],
                             r=g1_sb[:, fc * TOWN + i * TB: fc * TOWN + (i + 1) * TB],
                             s=(fc == 0), st=(fc == FC - 1):
                             e.matmul(pA[:], l, r, start=s, stop=st),
                             reads=[b_g1, b_wsl3[wsl]], writes=[b_pA])
                    p.op("vector", lambda e, bc=vec_sb[bnm][:, oc:oc + 1]:
                         e.tensor_scalar(pf3[:], pA[:], bc, None, ALU.add),
                         reads=[b_pA, b_const], writes=[b_pf3])
                    tcols = slice(i * TB, (i + 1) * TB)
                    if kind == "q":
                        rope2(tcols)
                        p.dma("sync", q2p[oc, :, tcols], pd3[:],
                              reads=[b_pd3], writes=[DB["q2p"]])
                    elif kind == "k":
                        rope2(tcols)
                        rb = oc * 512 + i * 128
                        p.dma("sync", k2src[rb:rb + 128, :], pd3[:],
                              reads=[b_pd3], writes=[DB["k2src"]])
                    else:
                        p.op("vector", lambda e: e.tensor_copy(pd3[:], pf3[:]),
                             reads=[b_pf3], writes=[b_pd3])
                        for kc in range(4):
                            p.op("tensor", lambda e, ii=pd3[:, kc * 128:(kc + 1) * 128]:
                                 e.transpose(pTb[:], ii, eye_sb[:]),
                                 reads=[b_pd3, b_const], writes=[b_pTb])
                            p.op("vector", lambda e, o=vst2[:, kc * 129:kc * 129 + 128]:
                                 e.tensor_copy(o, pTb[:]),
                                 reads=[b_pTb], writes=[b_vst2])
                        rb = ((oc * OWNT + i) * 4) * 128
                        p.dma("sync",
                              v2src[rb:rb + 512, :].rearrange("(kc p) c -> p kc c", p=128),
                              vst2[:].rearrange("p (kc c) -> p kc c", kc=4),
                              reads=[b_vst2], writes=[DB["v2src"]])
            if kind == "k":
                p.cc(lambda e: e.collective_compute(
                    "AllGather", ALU.bypass,
                    replica_groups=[[0, 1], [2, 3], [4, 5], [6, 7]],
                    ins=[k2src.opt()], outs=[k2g.opt()]),
                    reads=[DB["k2src"]], writes=[DB["k2g"]])
            elif kind == "v":
                p.cc(lambda e: e.collective_compute(
                    "AllGather", ALU.bypass,
                    replica_groups=[[0, 1], [2, 3], [4, 5], [6, 7]],
                    ins=[v2src.opt()], outs=[v2g.opt()]),
                    reads=[DB["v2src"]], writes=[DB["v2g"]])
        c3.close()

    def emit_p4():
        c4 = ExitStack()
        mat_sb = sb("mat_sb", [128, 4 * TB], BF16, c4)
        mbt_sb = sb("mbt_sb", [128, 4 * TB], BF16, c4)
        b_msk = Buf("msk", noww=True)
        p.dma("sync", mat_sb[:], mat_in, writes=[b_msk])
        p.dma("sync", mbt_sb[:], mbt_in, writes=[b_msk])
        K2b = sb("K2b", [128, 8 * TB], BF16, c4)
        b_K2 = [nb(f"K2s{s_}") for s_ in range(8)]
        V2tb = sb("V2tb", [128, 8 * 516], BF16, c4)
        b_V2 = [nb(f"V2s{s_}") for s_ in range(8)]
        q4 = sb("q4", [128, 2 * TB], BF16, c4)
        b_q4 = [nb("q4a"), nb("q4b")]
        Pb4 = sb("Pb4", [128, 3 * TB], BF16, c4)
        b_P4 = [nb(f"P4{j}") for j in range(3)]
        y4 = sb("y4", [128, TB], F32, c4); b_y4 = nb("y4")
        y2_acc = sb("y2_acc", [128, H1 * TOWN], BF16, c4)
        b_y2acc = Buf("y2acc", noww=True)
        pcnt4 = [0]
        for h in range(H1):
            for i in range(OWNT):
                for pos in (i, 4 + i):
                    rb = (pos // 4) * (H1 * 512) + h * 512 + (pos % 4) * 128
                    p.dma("sync", K2b[:, pos * TB:(pos + 1) * TB],
                          k2g[rb:rb + 128, :], reads=[DB["k2g"]], writes=[b_K2[pos]])
                    vb_ = (pos // 4) * (H1 * OWNT * 4 * 128) + ((h * OWNT + pos % 4) * 4) * 128
                    p.dma("sync",
                          V2tb[:, pos * 516:(pos + 1) * 516].rearrange("p (kc c) -> p kc c", kc=4),
                          v2g[vb_:vb_ + 512, :].rearrange("(kc p) c -> p kc c", p=128),
                          reads=[DB["v2g"]], writes=[b_V2[pos]])
                qsl = i % 2
                qv = q4[:, qsl * TB:(qsl + 1) * TB]
                p.dma("sync", qv, q2p[h, :, i * TB:(i + 1) * TB],
                      reads=[DB["q2p"]], writes=[b_q4[qsl]])
                POS = list(range(0, i + 1)) + list(range(4, 5 + i))
                first = True
                for idx, pos in enumerate(POS):
                    for kc in range(4):
                        sp_, b_sp = ((pA, b_pA) if (idx * 4 + kc) % 2 == 0
                                     else (pB, b_pB))
                        p.op("tensor", lambda e, o=sp_,
                             l=K2b[:, pos * TB + kc * 128: pos * TB + (kc + 1) * 128],
                             r=qv: e.matmul(o[:], l, r, start=True, stop=True),
                             reads=[b_K2[pos], b_q4[qsl]], writes=[b_sp])
                        pj = pcnt4[0] % 3
                        pcnt4[0] += 1
                        Pv = Pb4[:, pj * TB:(pj + 1) * TB]
                        if pos == i:
                            bias = bA2_sb[:, i:i + 1]
                        elif pos == 4 + i:
                            bias = bB2_sb[:, i:i + 1]
                        else:
                            bias = -C0B
                        p.op("scalar", lambda e, o=Pv, ii=sp_, bi=bias:
                             e.activation(o, ii[:], AF.Exp, bias=bi, scale=SCL),
                             reads=[b_sp, b_const], writes=[b_P4[pj]])
                        if pos == i or pos == 4 + i:
                            msk = mat_sb if pos == i else mbt_sb
                            meng = "vector" if (h + i) % 2 == 0 else "gpsimd"
                            p.op(meng, lambda e, o=Pv, m=msk[:, kc * TB:(kc + 1) * TB]:
                                 e.tensor_tensor(o, o, m, ALU.mult),
                                 reads=[b_P4[pj], b_msk], writes=[b_P4[pj]])
                        last = (idx == len(POS) - 1 and kc == 3)
                        vb2 = pos * 516 + kc * 129
                        p.op("tensor", lambda e, l=V2tb[:, vb2:vb2 + 65], r=Pv,
                             s=first, st=last:
                             e.matmul(pLO[0:65, :], l, r, start=s, stop=st),
                             reads=[b_V2[pos], b_P4[pj]], writes=[b_pLO])
                        p.op("tensor", lambda e, l=V2tb[:, vb2 + 64:vb2 + 129], r=Pv,
                             s=first, st=last:
                             e.matmul(pHI[0:65, :], l, r, start=s, stop=st),
                             reads=[b_V2[pos], b_P4[pj]], writes=[b_pHI])
                        first = False
                p.op("vector", lambda e: e.tensor_copy(y4[0:64, :], pLO[0:64, :]),
                     reads=[b_pLO], writes=[b_y4])
                p.op("vector", lambda e: e.tensor_copy(y4[64:128, :], pHI[0:64, :]),
                     reads=[b_pHI], writes=[b_y4])
                p.op("vector", lambda e: e.tensor_copy(R_X, pHI[64:65, :]),
                     reads=[b_pHI], writes=[b_rowscr])
                p.op("vector", lambda e: e.reciprocal(R_X, R_X),
                     reads=[b_rowscr], writes=[b_rowscr])
                p.op("tensor", lambda e: e.matmul(pC[:], ones_rf[:], R_X,
                                                  start=True, stop=True),
                     reads=[b_rowscr, b_const], writes=[b_pC])
                p.op("vector", lambda e,
                     o=y2_acc[:, h * TOWN + i * TB: h * TOWN + (i + 1) * TB]:
                     e.tensor_tensor(o, y4[:], pC[:], ALU.mult),
                     reads=[b_y4, b_pC], writes=[b_y2acc])
        return c4, y2_acc, b_y2acc

    def emit_p56(y2_acc, b_y2acc):
        c5 = ExitStack()
        wsl5 = sb("wsl5", [128, 2 * D], BF16, c5)
        b_wsl5 = [nb("w5a"), nb("w5b")]
        stat5 = sb("stat5", [1, 8 * TB], F32, c5); b_st5 = nb("stat5")
        mubc = sb("mubc", [128, OWNT * TB], F32, c5); b_mubc = Buf("mubc", noww=True)
        rsbc = sb("rsbc", [128, OWNT * TB], F32, c5); b_rsbc = Buf("rsbc", noww=True)
        xc5 = sb("xc5", [128, 2 * TB], F32, c5)
        b_xc5 = [nb("x5a"), nb("x5b")]
        asb = sb("asb", [128, TB], F32, c5); b_asb = nb("asb")
        hsb = sb("hsb", [128, TB], F32, c5); b_hsb = nb("hsb")
        hb5 = sb("hb5", [128, TB], BF16, c5); b_hb5 = nb("hb5")
        sq5 = sb("sq5", [128, TB], BF16, c5); b_sq5 = nb("sq5")
        for i in range(OWNT):
            p.dma("sync", R_L, rows_d[0:1, i * TB:(i + 1) * TB],
                  reads=[DB["rows_d"]], writes=[b_rowscr])
            p.dma("sync", R_MR, rows_d[1:2, i * TB:(i + 1) * TB],
                  reads=[DB["rows_d"]], writes=[b_rowscr])
            p.op("tensor", lambda e: e.matmul(pA[:], ones_rf[:], R_L, start=True, stop=True),
                 reads=[b_rowscr, b_const], writes=[b_pA])
            p.op("tensor", lambda e: e.matmul(pB[:], ones_rf[:], R_MR, start=True, stop=True),
                 reads=[b_rowscr, b_const], writes=[b_pB])
            p.op("vector", lambda e, o=mubc[:, i * TB:(i + 1) * TB]: e.tensor_copy(o, pA[:]),
                 reads=[b_pA], writes=[b_mubc])
            p.op("vector", lambda e, o=rsbc[:, i * TB:(i + 1) * TB]: e.tensor_copy(o, pB[:]),
                 reads=[b_pB], writes=[b_rsbc])
        for oc in range(FC):
            wsl = oc % 2
            p.dma("gpsimd", wsl5[:, wsl * D:(wsl + 1) * D], wo2[oc],
                  writes=[b_wsl5[wsl]])
            for i in range(OWNT):
                for fc in range(FC):
                    p.op("tensor", lambda e,
                         l=wsl5[:, wsl * D + fc * 128: wsl * D + (fc + 1) * 128],
                         r=y2_acc[:, fc * TOWN + i * TB: fc * TOWN + (i + 1) * TB],
                         s=(fc == 0), st=(fc == FC - 1):
                         e.matmul(pC[:], l, r, start=s, stop=st),
                         reads=[b_y2acc, b_wsl5[wsl]], writes=[b_pC])
                p.op("vector", lambda e, bc=vec_sb["bo2"][:, oc:oc + 1]:
                     e.tensor_scalar(asb[:], pC[:], bc, None, ALU.add),
                     reads=[b_pC, b_const], writes=[b_asb])
                xsl = i % 2
                xv = xc5[:, xsl * TB:(xsl + 1) * TB]
                p.dma("sync", xv, x1p[oc, :, i * TB:(i + 1) * TB],
                      reads=[DB["x1p"]], writes=[b_xc5[xsl]])
                p.op("vector", lambda e, x_=xv, m=mubc[:, i * TB:(i + 1) * TB]:
                     e.tensor_tensor(hsb[:], x_, m, ALU.subtract),
                     reads=[b_xc5[xsl], b_mubc], writes=[b_hsb])
                p.op("vector", lambda e, rr=rsbc[:, i * TB:(i + 1) * TB]:
                     e.tensor_tensor(hsb[:], hsb[:], rr, ALU.mult),
                     reads=[b_rsbc], writes=[b_hsb])
                p.op("vector", lambda e: e.tensor_tensor(hsb[:], hsb[:], asb[:], ALU.add),
                     reads=[b_asb], writes=[b_hsb])
                p.dma("sync", hp[oc, :, i * TB:(i + 1) * TB], hsb[:],
                      reads=[b_hsb], writes=[DB["hp"]])
                p.op("vector", lambda e: e.tensor_copy(hb5[:], hsb[:]),
                     reads=[b_hsb], writes=[b_hb5])
                p.op("vector", lambda e: e.tensor_tensor(sq5[:], hb5[:], hb5[:], ALU.mult),
                     reads=[b_hb5], writes=[b_sq5])
                p.op("tensor", lambda e: e.matmul(pD[0:1, :], ones_cb[:], hb5[:],
                                                  start=True, stop=True),
                     reads=[b_hb5, b_const], writes=[b_pD0])
                p.op("tensor", lambda e: e.matmul(pD[32:33, :], ones_cb[:], sq5[:],
                                                  start=True, stop=True),
                     reads=[b_sq5, b_const], writes=[b_pD1])
                s1v = stat5[0:1, i * TB:(i + 1) * TB]
                s2v = stat5[0:1, (4 + i) * TB:(5 + i) * TB]
                if oc == 0:
                    p.op("vector", lambda e, o=s1v: e.tensor_copy(o, pD[0:1, :]),
                         reads=[b_pD0], writes=[b_st5])
                    p.op("vector", lambda e, o=s2v: e.tensor_copy(o, pD[32:33, :]),
                         reads=[b_pD1], writes=[b_st5])
                else:
                    p.op("vector", lambda e, o=s1v: e.tensor_tensor(o, o, pD[0:1, :], ALU.add),
                         reads=[b_pD0], writes=[b_st5])
                    p.op("vector", lambda e, o=s2v: e.tensor_tensor(o, o, pD[32:33, :], ALU.add),
                         reads=[b_pD1], writes=[b_st5])
        for i in range(OWNT):
            s1v = stat5[0:1, i * TB:(i + 1) * TB]
            s2v = stat5[0:1, (4 + i) * TB:(5 + i) * TB]
            p.op("vector", lambda e, s_=s1v: e.tensor_scalar(R_MU, s_, 1.0 / D, None, ALU.mult),
                 reads=[b_st5], writes=[b_rowscr])
            p.op("vector", lambda e, s_=s2v: e.tensor_scalar(R_T, s_, 1.0 / D, None, ALU.mult),
                 reads=[b_st5], writes=[b_rowscr])
            p.op("vector", lambda e: e.tensor_tensor(R_VAR, R_MU, R_MU, ALU.mult),
                 reads=[b_rowscr], writes=[b_rowscr])
            p.op("vector", lambda e: e.tensor_tensor(R_VAR, R_T, R_VAR, ALU.subtract),
                 reads=[b_rowscr], writes=[b_rowscr])
            p.op("scalar", lambda e: e.activation(R_L, R_VAR, AF.Ln, bias=EPS_LN, scale=1.0),
                 reads=[b_rowscr], writes=[b_rowscr])
            p.op("scalar", lambda e: e.activation(R_RS, R_L, AF.Exp, bias=0.0, scale=-0.5),
                 reads=[b_rowscr], writes=[b_rowscr])
            p.op("vector", lambda e: e.tensor_tensor(R_MR, R_MU, R_RS, ALU.mult),
                 reads=[b_rowscr], writes=[b_rowscr])
            p.dma("sync", rows_d[2:3, i * TB:(i + 1) * TB], R_RS,
                  reads=[b_rowscr], writes=[DB["rows_d"]])
            p.dma("sync", rows_d[3:4, i * TB:(i + 1) * TB], R_MR,
                  reads=[b_rowscr], writes=[DB["rows_d"]])
        gt5 = sb("gt5", [128, TB], F32, c5); b_gt5 = nb("gt5")
        for i in range(OWNT):
            p.dma("sync", R_RS, rows_d[2:3, i * TB:(i + 1) * TB],
                  reads=[DB["rows_d"]], writes=[b_rowscr])
            p.dma("sync", R_MR, rows_d[3:4, i * TB:(i + 1) * TB],
                  reads=[DB["rows_d"]], writes=[b_rowscr])
            p.op("tensor", lambda e: e.matmul(pA[:], ones_rf[:], R_RS, start=True, stop=True),
                 reads=[b_rowscr, b_const], writes=[b_pA])
            p.op("tensor", lambda e: e.matmul(pB[:], ones_rf[:], R_MR, start=True, stop=True),
                 reads=[b_rowscr, b_const], writes=[b_pB])
            for fc in range(FC):
                xsl = fc % 2
                xv = xc5[:, xsl * TB:(xsl + 1) * TB]
                p.dma("sync", xv, hp[fc, :, i * TB:(i + 1) * TB],
                      reads=[DB["hp"]], writes=[b_xc5[xsl]])
                p.op("vector", lambda e, x_=xv: e.tensor_tensor(gt5[:], x_, pA[:], ALU.mult),
                     reads=[b_xc5[xsl], b_pA], writes=[b_gt5])
                p.op("vector", lambda e: e.tensor_tensor(gt5[:], gt5[:], pB[:], ALU.subtract),
                     reads=[b_pB], writes=[b_gt5])
                p.op("vector", lambda e, w1=vec_sb["l2w"][:, fc:fc + 1],
                     b1=vec_sb["l2b"][:, fc:fc + 1]:
                     e.tensor_scalar(hb5[:], gt5[:], w1, b1, ALU.mult, ALU.add),
                     reads=[b_gt5, b_const], writes=[b_hb5])
                p.dma("sync", g2p[fc, :, i * TB:(i + 1) * TB], hb5[:],
                      reads=[b_hb5], writes=[DB["g2p"]])
        c5.close()

    def emit_p7():
        c7 = ExitStack()
        g2s = sb("g2s", [128, FC * TB], BF16, c7)
        b_g2s = [nb(f"g2s{fc}") for fc in range(FC)]
        hmid = sb("hmid", [128, E * IC * TB], BF16, c7)
        b_hmid = nb("hmid")
        w1sl = sb("w1sl", [128, 2 * D], BF16, c7)
        b_w1sl = [nb("w1a"), nb("w1b")]
        w2sl = sb("w2sl", [128, 8 * D], BF16, c7)
        b_w2sl = [nb(f"w2s{j}") for j in range(8)]
        be2s = sb("be2s", [E, D], BF16, c7)
        b_be2 = nb("be2s")
        p.dma("gpsimd", be2s[:], be2, writes=[b_be2])
        sel_sb = sb("sel_sb", [E, E * 128], F32, c7)
        p.dma("sync", sel_sb[:], sel_h.ap(), writes=[b_const])
        lg5 = sb("lg5", [4, TB], F32, c7); b_lg = nb("lg5")
        gsc = sb("gsc", [128, 48], F32, c7); b_gsc = nb("gsc")
        ctm5 = sb("ctm5", [128, 4], F32, c7); b_ctm = nb("ctm5")
        comb_fm = sb("comb_fm", [4, TB], F32, c7); b_cfm = nb("cfm")
        combb = sb("combb", [4, TB], BF16, c7); b_cbb = nb("cbb")
        cbc = sb("cbc", [128, E * TB], F32, c7); b_cbc = Buf("cbc", noww=True)
        gel = sb("gel", [128, TB], F32, c7); b_gel = nb("gel")
        osb = sb("osb", [128, TB], F32, c7); b_osb = nb("osb")
        xc7 = sb("xc7", [128, 2 * TB], F32, c7)
        b_xc7 = [nb("x7a"), nb("x7b")]
        hc7 = sb("hc7", [128, 2 * TB], F32, c7)
        b_hc7 = [nb("h7a"), nb("h7b")]
        for s_ in range(OWNT):
            tcols = slice(s_ * TB, (s_ + 1) * TB)
            for fc in range(FC):
                p.dma("sync", g2s[:, fc * TB:(fc + 1) * TB], g2p[fc, :, tcols],
                      reads=[DB["g2p"]], writes=[b_g2s[fc]])
            for fc in range(FC):
                p.op("tensor", lambda e, l=wg_sb[:, fc * E:(fc + 1) * E],
                     r=g2s[:, fc * TB:(fc + 1) * TB], s=(fc == 0), st=(fc == FC - 1):
                     e.matmul(pC[0:E, :], l, r, start=s, stop=st),
                     reads=[b_g2s[fc], b_const], writes=[b_pC])
            p.op("vector", lambda e: e.tensor_copy(lg5[:], pC[0:E, :]),
                 reads=[b_pC], writes=[b_lg])
            for cc_ in range(4):
                p.op("tensor", lambda e, ii=lg5[0:4, cc_ * 128:(cc_ + 1) * 128]:
                     e.transpose(pTf[0:128, 0:4], ii, eyef_sb[0:4, 0:4]),
                     reads=[b_lg, b_const], writes=[b_pTf])
                G = lambda j: gsc[:, 4 * j:4 * (j + 1)]
                C1 = lambda j: gsc[:, 32 + j:33 + j]
                p.op("vector", lambda e: e.tensor_copy(G(0), pTf[0:128, 0:4]),
                     reads=[b_pTf], writes=[b_gsc])
                p.op("vector", lambda e: e.tensor_reduce(C1(0), G(0), AX.X, ALU.max, negate=True),
                     reads=[b_gsc], writes=[b_gsc])
                p.op("scalar", lambda e: e.activation(G(1), G(0), AF.Exp, bias=C1(0), scale=1.0),
                     reads=[b_gsc], writes=[b_gsc])
                p.op("vector", lambda e: e.tensor_reduce(C1(1), G(1), AX.X, ALU.add),
                     reads=[b_gsc], writes=[b_gsc])
                p.op("vector", lambda e: e.reciprocal(C1(2), C1(1)),
                     reads=[b_gsc], writes=[b_gsc])
                p.op("vector", lambda e: e.tensor_scalar(G(2), G(1), C1(2), None, ALU.mult),
                     reads=[b_gsc], writes=[b_gsc])
                p.op("vector", lambda e: e.tensor_reduce(C1(3), G(2), AX.X, ALU.max),
                     reads=[b_gsc], writes=[b_gsc])
                p.op("vector", lambda e: e.tensor_scalar(G(3), G(2), C1(3), None, ALU.is_ge),
                     reads=[b_gsc], writes=[b_gsc])
                p.op("vector", lambda e: e.scalar_tensor_tensor(G(4), G(3), NEGB, G(2), ALU.mult, ALU.add),
                     reads=[b_gsc], writes=[b_gsc])
                p.op("vector", lambda e: e.tensor_reduce(C1(4), G(4), AX.X, ALU.max),
                     reads=[b_gsc], writes=[b_gsc])
                p.op("vector", lambda e: e.tensor_scalar(G(5), G(2), C1(4), None, ALU.is_ge),
                     reads=[b_gsc], writes=[b_gsc])
                p.op("scalar", lambda e: e.activation(G(6), G(2), AF.Exp, bias=0.0, scale=1.0),
                     reads=[b_gsc], writes=[b_gsc])
                p.op("vector", lambda e: e.tensor_tensor(G(6), G(6), G(5), ALU.mult),
                     reads=[b_gsc], writes=[b_gsc])
                p.op("vector", lambda e: e.tensor_reduce(C1(5), G(6), AX.X, ALU.add),
                     reads=[b_gsc], writes=[b_gsc])
                p.op("vector", lambda e: e.reciprocal(C1(6), C1(5)),
                     reads=[b_gsc], writes=[b_gsc])
                p.op("vector", lambda e: e.tensor_scalar(ctm5[:], G(6), C1(6), None, ALU.mult),
                     reads=[b_gsc], writes=[b_ctm])
                if "gsc" in debug and cc_ == 0:
                    if s_ == 0:
                        emit_p7.dbg_gsc = nc.dram_tensor(
                            "dbg_gsc", [OWNT * 128, 48], F32, kind="ExternalOutput").ap()
                    p.dma("sync", emit_p7.dbg_gsc[s_ * 128:(s_ + 1) * 128, :], gsc[:],
                          reads=[b_gsc], writes=[Buf("dbggsc")])
                p.op("tensor", lambda e: e.transpose(pTf[0:4, 0:128], ctm5[:], eyef_sb[:]),
                     reads=[b_ctm, b_const], writes=[b_pTf])
                p.op("vector", lambda e, o=comb_fm[0:4, cc_ * 128:(cc_ + 1) * 128]:
                     e.tensor_copy(o, pTf[0:4, 0:128]),
                     reads=[b_pTf], writes=[b_cfm])
            if "comb" in debug:
                if s_ == 0:
                    emit_p7.dbg_comb = nc.dram_tensor(
                        "dbg_comb", [4 * OWNT, TB], F32, kind="ExternalOutput").ap()
                p.dma("sync", emit_p7.dbg_comb[s_ * 4:(s_ + 1) * 4, :], comb_fm[:],
                      reads=[b_cfm], writes=[Buf("dbgcomb")])
            p.op("vector", lambda e: e.tensor_copy(combb[:], comb_fm[:]),
                 reads=[b_cfm], writes=[b_cbb])
            for e_ in range(E):
                p.op("tensor", lambda e, l=sel_sb[0:4, e_ * 128:(e_ + 1) * 128]:
                     e.matmul(pC[:], l, comb_fm[:], start=True, stop=True),
                     reads=[b_cfm, b_const], writes=[b_pC])
                p.op("vector", lambda e, o=cbc[:, e_ * TB:(e_ + 1) * TB]:
                     e.tensor_copy(o, pC[:]), reads=[b_pC], writes=[b_cbc])
            for e_ in range(E):
                for ic in range(IC):
                    wsl = (e_ * IC + ic) % 2
                    p.dma("gpsimd", w1sl[:, wsl * D:(wsl + 1) * D], we1[e_, ic],
                          writes=[b_w1sl[wsl]])
                    for fc in range(FC):
                        p.op("tensor", lambda e,
                             l=w1sl[:, wsl * D + fc * 128: wsl * D + (fc + 1) * 128],
                             r=g2s[:, fc * TB:(fc + 1) * TB],
                             s=(fc == 0), st=(fc == FC - 1):
                             e.matmul(pA[:], l, r, start=s, stop=st),
                             reads=[b_g2s[fc], b_w1sl[wsl]], writes=[b_pA])
                        p.op("scalar", lambda e, bi=be1_sb[:, e_ * IC + ic: e_ * IC + ic + 1]:
                             e.activation(gel[:], pA[:], AF.Gelu, bias=bi, scale=1.0),
                             reads=[b_pA, b_const], writes=[b_gel]) if fc == FC - 1 else None
                    p.op("vector", lambda e,
                         o=hmid[:, (e_ * IC + ic) * TB:(e_ * IC + ic + 1) * TB],
                         cb=cbc[:, e_ * TB:(e_ + 1) * TB]:
                         e.tensor_tensor(o, gel[:], cb, ALU.mult),
                         reads=[b_gel, b_cbc], writes=[b_hmid])
            for oc in range(FC):
                for e_ in range(E):
                    j = (oc % 2) * 4 + e_
                    p.dma("gpsimd", w2sl[:, j * D:(j + 1) * D], we2[e_, oc],
                          writes=[b_w2sl[j]])
                for e_ in range(E):
                    j = (oc % 2) * 4 + e_
                    for ic in range(IC):
                        p.op("tensor", lambda e,
                             l=w2sl[:, j * D + ic * 128: j * D + (ic + 1) * 128],
                             r=hmid[:, (e_ * IC + ic) * TB:(e_ * IC + ic + 1) * TB],
                             s=(e_ == 0 and ic == 0):
                             e.matmul(pB[:], l, r, start=s, stop=False),
                             reads=[b_hmid, b_w2sl[j]], writes=[b_pB])
                p.op("tensor", lambda e, l=be2s[0:4, oc * 128:(oc + 1) * 128]:
                     e.matmul(pB[:], l, combb[:], start=False, stop=True),
                     reads=[b_be2, b_cbb], writes=[b_pB])
                xsl = oc % 2
                xv = xc7[:, xsl * TB:(xsl + 1) * TB]
                hv = hc7[:, xsl * TB:(xsl + 1) * TB]
                p.dma("sync", xv, x1p[oc, :, tcols], reads=[DB["x1p"]], writes=[b_xc7[xsl]])
                p.dma("sync", hv, hp[oc, :, tcols], reads=[DB["hp"]], writes=[b_hc7[xsl]])
                p.op("vector", lambda e, x_=xv: e.tensor_tensor(osb[:], pB[:], x_, ALU.add),
                     reads=[b_pB, b_xc7[xsl]], writes=[b_osb])
                p.op("vector", lambda e, h_=hv: e.tensor_tensor(osb[:], osb[:], h_, ALU.add),
                     reads=[b_hc7[xsl]], writes=[b_osb])
                p.dma("sync", OUT[oc * 128:(oc + 1) * 128, tcols], osb[:],
                      reads=[b_osb], writes=[DB["out"]])
        c7.close()

    if phase_limit >= 2:
        ctx3, g1_sb, b_g1 = emit_p23()
        if phase_limit >= 3:
            emit_p3b(g1_sb, b_g1)
        ctx3.close()
        p.barrier()
        if phase_limit >= 4:
            p4 = emit_p4()
            c4, y2_acc, b_y2acc = p4
            if phase_limit >= 5:
                emit_p56(y2_acc, b_y2acc)
            c4.close()
            p.barrier()
            if phase_limit >= 7:
                emit_p7()
    if phase_limit < 7:
        p.dma("sync", OUT[0:128, 0:len(CONSTS)], cp_sb[:], reads=[b_const],
              writes=[DB["out"]])

    dbg_map = {"q1p": (q1p, 1), "k1p": (k1p, 1), "vnp": (vnp, 1), "v1tp": (v1tp, 0),
               "x1p": (x1p, 1), "q2p": (q2p, 1), "k2g": (k2g, 0), "v2g": (v2g, 0),
               "hp": (hp, 1), "g2p": (g2p, 1), "rows_d": (rows_d, 0)}
    if "nodbg" in _KSKIP:
        debug = ()
    for nm in debug:
        if nm not in dbg_map:
            continue
        src, flatten = dbg_map[nm]
        flat = src.rearrange("a b c -> (a b) c") if flatten else src
        o = nc.dram_tensor("dbg_" + nm, [flat.shape[0], flat.shape[1]],
                           src.tensor.dtype, kind="ExternalOutput").ap()
        p.dma("sync", o, flat, reads=[DB[nm]], writes=[Buf("dbg")])
    if "nobar2" not in _KSKIP:
        p.barrier()

    if "nomark" not in _KSKIP:
        w = p._mkop("sync", None, "y", 0)
        w.waits = [("ABS", p.sems[f"q{j}"], p.counts[f"q{j}"] * 16)
                   for j in range(QPOOL) if p.counts[f"q{j}"]]
        w.waits.append(("ABS", p.sems["g"], p.counts["g"] * 16))
        if p.counts["cc"]:
            w.waits.append(("ABS", p.sems["cc"], p.counts["cc"]))

    p.finish()
    ctx.close()
    return nc


def _build_packs(I):
    pk = {}
    pk["wq1"] = _pack_col(np.asarray(I["L0_attn_Wq"]), H1, FC)
    pk["wk1"] = _pack_col(np.asarray(I["L0_attn_Wk"]), KVH1, FC)
    pk["wv1"] = _pack_col(np.asarray(I["L0_attn_Wv"]), KVH1, FC)
    pk["wpj"] = _pack_col(np.asarray(I["L0_attn_Wproj"]), FC, FC)
    pk["wq2"] = _pack_col(np.asarray(I["L0_ffn_wq"]), FC, FC)
    pk["wk2"] = _pack_col(np.asarray(I["L0_ffn_wk"]), FC, FC)
    pk["wv2"] = _pack_col(np.asarray(I["L0_ffn_wv"]), FC, FC)
    pk["wo2"] = _pack_col(np.asarray(I["L0_ffn_wo"]), FC, FC)
    pk["qa"] = _pack_A(np.asarray(I["L0_attn_q_A"])); pk["qb"] = np.asarray(I["L0_attn_q_B"])
    pk["ka"] = _pack_A(np.asarray(I["L0_attn_k_A"])); pk["kb"] = np.asarray(I["L0_attn_k_B"])
    pk["va"] = _pack_A(np.asarray(I["L0_attn_v_A"])); pk["vb"] = np.asarray(I["L0_attn_v_B"])
    pk["pa"] = _pack_A(np.asarray(I["L0_attn_p_A"])); pk["pb"] = np.asarray(I["L0_attn_p_B"])
    pk["gain"] = np.ascontiguousarray(np.asarray(I["L0_attn_q_gain"]).reshape(1, H1))
    for nm, key in (("bq2", "L0_ffn_bq"), ("bk2", "L0_ffn_bk"), ("bv2", "L0_ffn_bv"),
                    ("bo2", "L0_ffn_bo"), ("l1w", "L0_ffn_ln1_w"), ("l1b", "L0_ffn_ln1_b"),
                    ("l2w", "L0_ffn_ln2_w"), ("l2b", "L0_ffn_ln2_b")):
        pk[nm] = _pack_vec(np.asarray(I[key]))
    pk["wg"] = np.ascontiguousarray(
        np.asarray(I["L0_ffn_w_gate"]).reshape(FC, 128, E).transpose(1, 0, 2).reshape(128, FC * E))
    we1 = np.asarray(I["L0_ffn_we1"])
    pk["we1"] = np.ascontiguousarray(
        we1.reshape(E, FC, 128, IC, 128).transpose(0, 3, 2, 1, 4).reshape(E, IC, 128, D))
    pk["be1"] = np.ascontiguousarray(
        np.asarray(I["L0_ffn_be1"]).reshape(E, IC, 128).transpose(2, 0, 1).reshape(128, E * IC))
    we2 = np.asarray(I["L0_ffn_we2"])
    pk["we2"] = np.ascontiguousarray(
        we2.reshape(E, IC, 128, FC, 128).transpose(0, 3, 2, 1, 4).reshape(E, FC, 128, I_DIM))
    pk["be2"] = np.asarray(I["L0_ffn_be2"])
    return pk


def _stage_core(I, pk, b, r):
    own = SNAKE[r]
    sib = SNAKE[1 - r]
    loc = own + sib
    tok = np.concatenate([np.arange(g * TB, (g + 1) * TB) for g in loc])
    d = dict(pk)
    x = np.asarray(I["L0_attn_x"][b])
    d["xfm"] = np.ascontiguousarray(x[tok].T)

    pos = tok.astype(np.float64)
    inv1 = 10000.0 ** (-np.arange(0, ROPE1, 2, dtype=np.float64) / ROPE1)
    fr1 = inv1[:, None] * pos[None, :]
    d["cos1"] = np.cos(fr1).astype(np.float32)
    d["sin1"] = np.sin(fr1).astype(np.float32)
    pos2 = pos[:TOWN]
    inv2 = 10000.0 ** (-np.arange(0, HD, 2, dtype=np.float64) / HD)
    fr2 = inv2[:, None] * pos2[None, :]
    d["cos2"] = np.cos(fr2).astype(np.float32)
    d["sin2"] = np.sin(fr2).astype(np.float32)

    bB1 = np.array([(-C0A if sib[i] < own[i] else NEGB) for i in range(4)], np.float32)
    d["biasB1"] = np.ascontiguousarray(np.tile(bB1[None, :], (128, 1)))
    ev, od = SNAKE[0], SNAKE[1]
    if r == 0:
        bA2 = np.full(4, -C0B, np.float32)
        bB2 = np.array([(-C0B if od[i] < ev[i] else NEGB) for i in range(4)], np.float32)
        mat = np.asarray(_tri_np(), dtype=np.float32)
        mbt = np.ones((128, 4 * TB), np.float32)
    else:
        bA2 = np.array([(-C0B if ev[i] < od[i] else NEGB) for i in range(4)], np.float32)
        bB2 = np.full(4, -C0B, np.float32)
        mat = np.ones((128, 4 * TB), np.float32)
        mbt = np.asarray(_tri_np(), dtype=np.float32)
    d["biasA2"] = np.ascontiguousarray(np.tile(bA2[None, :], (128, 1)))
    d["biasB2"] = np.ascontiguousarray(np.tile(bB2[None, :], (128, 1)))
    d["mat_in"] = np.ascontiguousarray(mat.astype(BB))
    d["mbt_in"] = np.ascontiguousarray(mbt.astype(BB))
    return d, tok


def _run(inputs, debug=(), phase_limit=None):
    pk = _build_packs(inputs)
    nc = build_program(phase_limit=phase_limit, debug=debug)
    in_maps = []
    toks = []
    for c in range(8):
        d, tok = _stage_core(inputs, pk, c // 2, c % 2)
        in_maps.append(d)
        toks.append(tok)
    res = run_bass_kernel_spmd(nc, in_maps, core_ids=list(range(8)))
    return res, toks


def kernel(**inputs):
    res, toks = _run(inputs, debug=DEBUG_DUMPS)
    out = np.empty((B, T, D), np.float32)
    for c in range(8):
        tok_own = toks[c][:TOWN]
        out[c // 2, tok_own, :] = np.asarray(res.results[c]["out"]).T
    return out


# version 06
# speedup vs baseline: 1.0012x; 1.0012x over previous
import math
import os
from contextlib import ExitStack
_KSKIP = set(os.environ.get("KSKIP", "").split(","))

import numpy as np
import ml_dtypes
import concourse.bass as bass
import concourse.mybir as mybir
from concourse.bass_utils import run_bass_kernel_spmd

F32 = mybir.dt.float32
BF16 = mybir.dt.bfloat16
AF = mybir.ActivationFunctionType
ALU = mybir.AluOpType
AX = mybir.AxisListType
BB = ml_dtypes.bfloat16

B, T, D = 4, 4096, 2048
FC = 16
HD = 128
H1, KVH1, G1 = 16, 4, 4
KV_DIM = 512
RANK = 16
ROPE1 = 64
E, I_DIM, IC = 4, 2048, 16
TB = 512
NT = 8
OWNT = 4
TOWN = OWNT * TB
C0A = 12.0
C0B = 15.0
NEGB = -1e4
EPS_LN = 1e-5
EPS_RMS = float(np.finfo(np.float32).eps)
SCL = HD ** -0.5
SNAKE = ([0, 3, 4, 7], [1, 2, 5, 6])
PHASE_LIMIT = 99
DEBUG_DUMPS = ()
PARANOID = 0

ENGS = ("tensor", "vector", "scalar", "gpsimd", "sync")
ESEM = {"tensor": "t", "vector": "v", "scalar": "s", "gpsimd": "p", "sync": "y"}
QSEM = {"sync": "q", "gpsimd": "g"}


def _pack_col(W, ocn, fcn):
    return np.ascontiguousarray(
        W.reshape(fcn, 128, ocn, 128).transpose(2, 1, 0, 3).reshape(ocn, 128, fcn * 128))


def _pack_A(A):
    return np.ascontiguousarray(
        A.reshape(FC, 128, RANK).transpose(1, 0, 2).reshape(128, FC * RANK))


def _pack_vec(v):
    return np.ascontiguousarray(v.reshape(FC, 128).T)


def _tri_np():
    out = np.zeros((128, 4 * TB), np.float32)
    q = np.arange(TB)
    for kc in range(4):
        k = kc * 128 + np.arange(128)
        out[:, kc * TB:(kc + 1) * TB] = (k[:, None] <= q[None, :])
    return out.astype(BB)


class _Sem:
    __slots__ = ("h", "name")

    def __init__(self, h, name):
        self.h = h
        self.name = name


class _Op:
    __slots__ = ("eng", "fn", "sem", "idx", "needed", "waits", "amt", "cval",
                 "psrc")

    def __init__(self, eng, fn, sem, idx, amt):
        self.eng = eng
        self.fn = fn
        self.sem = sem
        self.idx = idx
        self.needed = False
        self.waits = []
        self.amt = amt
        self.cval = None
        self.psrc = False


class Buf:
    __slots__ = ("name", "w", "r", "noww", "is_psum")

    def __init__(self, name, noww=False, is_psum=False):
        self.name = name
        self.w = {}
        self.r = {}
        self.noww = noww
        self.is_psum = is_psum


QPOOL = 6


class Prog:
    def __init__(self, nc, ctx):
        self.nc = nc
        self.ctx = ctx
        self.streams = {e: [] for e in ENGS}
        self.sems = {}
        names = ["t", "v", "s", "p", "y", "g", "cc", "bar"] + \
            [f"q{j}" for j in range(QPOOL)] + ["g0", "g1", "g2"]
        for nm in names:
            h = ctx.enter_context(nc.semaphore("sem_" + nm))
            self.sems[nm] = _Sem(h, nm)
        self.counts = {nm: 0 for nm in self.sems}
        self.qrr = 0
        self.grr = 0

    def _mkop(self, eng, fn, semname, amt, always=False):
        sem = self.sems[semname]
        self.counts[semname] += 1
        op = _Op(eng, fn, sem, self.counts[semname], amt)
        if always:
            op.needed = True
        self.streams[eng].append(op)
        return op

    def _deps(self, reads, writes):
        deps = []
        for b in reads:
            deps.extend(b.w.values())
        for b in writes:
            if not b.noww:
                deps.extend(b.w.values())
            deps.extend(b.r.values())
        return deps

    def _track(self, op, reads, writes, semname):
        for b in writes:
            if b.noww:
                b.w[semname] = op
            else:
                b.w = {semname: op}
                b.r = {}
        for b in reads:
            b.r[semname] = op

    def op(self, eng, fn, reads=(), writes=()):
        semname = ESEM[eng]
        deps = self._deps(reads, writes)
        op = self._mkop(eng, fn, semname, 1)
        op.psrc = any(b.is_psum for b in reads)
        for d in deps:
            d.needed = True
            op.waits.append(d)
        self._track(op, reads, writes, semname)
        return op

    def dma(self, eng, out_ap, in_ap, reads=(), writes=()):
        deps = self._deps(reads, writes)
        if eng == "gpsimd":
            semname = f"g{self.grr % 3}"
            self.grr += 1
        else:
            semname = f"q{self.qrr % QPOOL}"
            self.qrr += 1
        prev = self.counts[semname]
        if prev:
            m = self._mkop(eng, None, ESEM[eng], 0)
            m.waits = [("ABS", self.sems[semname], prev * 16)]
        op = self._mkop(eng, lambda e, o=out_ap, i=in_ap: e.dma_start(out=o, in_=i),
                        semname, 16, always=True)
        for d in deps:
            d.needed = True
            op.waits.append(d)
        self._track(op, reads, writes, semname)
        return op

    def cc(self, fn, reads=(), writes=()):
        deps = self._deps(reads, writes)
        op = self._mkop("gpsimd", fn, "cc", 1, always=True)
        for d in deps:
            d.needed = True
            op.waits.append(d)
        self._track(op, reads, writes, "cc")
        return op

    def barrier(self):
        bs = self.sems["bar"]
        for e in ENGS:
            self._mkop(e, lambda en: en.nop(), "bar", 1, always=True)
        tgt = self.counts["bar"]
        for e in ENGS:
            w = self._mkop(e, None, ESEM[e], 0)
            w.waits = [("ABS", bs, tgt)]
        return tgt

    def finish(self):
        sem_ops = {nm: [] for nm in self.sems}
        for e in ENGS:
            for op in self.streams[e]:
                if op.fn is not None:
                    sem_ops[op.sem.name].append(op)
        for nm, lst in sem_ops.items():
            lst.sort(key=lambda o: o.idx)
            c = 0
            for o in lst:
                if o.needed:
                    c += o.amt
                o.cval = c
        nc = self.nc
        with nc.Block() as block:
            for ename in ENGS:
                deco = getattr(block, ename)

                def body(e, ename=ename):
                    know = {}
                    for op in self.streams[ename]:
                        for d in op.waits:
                            if isinstance(d, tuple):
                                _, sem, val = d
                            else:
                                sem, val = d.sem, d.cval
                            if val and know.get(sem.name, 0) < val:
                                e.wait_ge(sem.h, val)
                                know[sem.name] = val
                        if op.fn is None:
                            continue
                        ins = op.fn(e)
                        if op.needed:
                            ins.then_inc(op.sem.h, op.amt)

                deco(body)


def build_program(phase_limit=None, debug=()):
    if phase_limit is None:
        phase_limit = PHASE_LIMIT
    nc = bass.Bass("TRN2", target_bir_lowering=False, debug=False)
    ctx = ExitStack()
    p = Prog(nc, ctx)

    def din(name, shape, dt=F32):
        return nc.dram_tensor(name, list(shape), dt, kind="ExternalInput").ap()

    def dint(name, shape, dt):
        return nc.dram_tensor(name, list(shape), dt, kind="Internal").ap()

    if "inX" in _KSKIP:
        din0 = din
        din = lambda name, shape, dt=F32: din0(name, [2, 2], dt)
    xfm = din("xfm", [D, T])
    wq1 = din("wq1", [H1, 128, D]); wk1 = din("wk1", [KVH1, 128, D])
    wv1 = din("wv1", [KVH1, 128, D]); wpj = din("wpj", [FC, 128, D])
    wq2 = din("wq2", [FC, 128, D]); wk2 = din("wk2", [FC, 128, D])
    wv2 = din("wv2", [FC, 128, D]); wo2 = din("wo2", [FC, 128, D])
    qa = din("qa", [128, FC * RANK]); qb = din("qb", [RANK, D])
    ka = din("ka", [128, FC * RANK]); kb = din("kb", [RANK, KV_DIM])
    va = din("va", [128, FC * RANK]); vb = din("vb", [RANK, KV_DIM])
    pa = din("pa", [128, FC * RANK]); pb = din("pb", [RANK, D])
    gain = din("gain", [1, H1])
    vecs = {nm: din(nm, [128, FC]) for nm in
            ("bq2", "bk2", "bv2", "bo2", "l1w", "l1b", "l2w", "l2b")}
    wg = din("wg", [128, FC * E])
    we1 = din("we1", [E, IC, 128, D])
    be1 = din("be1", [128, E * IC])
    we2 = din("we2", [E, FC, 128, I_DIM])
    be2 = din("be2", [E, D])
    cos1 = din("cos1", [ROPE1 // 2, T]); sin1 = din("sin1", [ROPE1 // 2, T])
    cos2 = din("cos2", [HD // 2, TOWN]); sin2 = din("sin2", [HD // 2, TOWN])
    biasB1 = din("biasB1", [128, OWNT])
    biasA2 = din("biasA2", [128, OWNT])
    biasB2 = din("biasB2", [128, OWNT])
    mat_in = din("mat_in", [128, 4 * TB], BF16)
    mbt_in = din("mbt_in", [128, 4 * TB], BF16)

    OUT = nc.dram_tensor("out", [D, TOWN], F32, kind="ExternalOutput").ap()

    eye_h = nc.inline_tensor(np.eye(128, dtype=BB), name="eye_h")
    eyef_h = nc.inline_tensor(np.eye(128, dtype=np.float32), name="eyef_h")
    tri_h = nc.inline_tensor(_tri_np(), name="tri_h")
    sel_np = np.zeros((E, E * 128), np.float32)
    for e_ in range(E):
        sel_np[e_, e_ * 128:(e_ + 1) * 128] = 1
    sel_h = nc.inline_tensor(sel_np, name="sel_h")
    CONSTS = sorted({0.0, 1.0, -1.0, 0.5, -0.5, 2.0, -C0A, -C0B, EPS_LN, EPS_RMS,
                     1.0 / D, 1.0 / HD, 1e-30, 1e-12, SCL, NEGB, 1.0 / I_DIM})
    cp_h = nc.inline_tensor(np.tile(np.array([CONSTS], np.float32), (128, 1)),
                            name="cp_h")

    if "dram" in _KSKIP:
        dint0 = dint
        dint = lambda name, shape, dt: dint0(name, [2, 2], dt)
    q1p = dint("q1p", [H1, 128, TOWN], BF16)
    k1p = dint("k1p", [KVH1, 128, T], BF16)
    v1tp = dint("v1tp", [KVH1 * NT * 4 * 128, 129], BF16)
    vnp = dint("vnp", [KVH1, 128, TOWN], F32)
    x1p = dint("x1p", [FC, 128, TOWN], F32)
    q2p = dint("q2p", [H1, 128, TOWN], BF16)
    k2src = dint("k2src", [H1 * 512, TB], BF16)
    k2g = dint("k2g", [2 * H1 * 512, TB], BF16)
    v2src = dint("v2src", [H1 * OWNT * 4 * 128, 129], BF16)
    v2g = dint("v2g", [2 * H1 * OWNT * 4 * 128, 129], BF16)
    hp = dint("hp", [FC, 128, TOWN], F32)
    g2p = dint("g2p", [FC, 128, TOWN], BF16)
    rows_d = dint("rows_d", [4, TOWN], F32)

    DB = {nm: Buf(nm, noww=True) for nm in
          ("q1p", "k1p", "v1tp", "vnp", "x1p", "q2p", "k2src", "k2g",
           "v2src", "v2g", "hp", "g2p", "rows_d", "out")}

    sb = lambda name, shape, dt, c=ctx: c.enter_context(nc.sbuf_tensor(name, list(shape), dt))
    psl = lambda name, shape, dt, c=ctx: c.enter_context(nc.psum_tensor(name, list(shape), dt))

    if "sbufres" in _KSKIP:
        sb0 = sb
        sb = lambda name, shape, dt, c=ctx: sb0(name, [shape[0], 2] if shape[1] > 2 else shape, dt, c)
    cp_sb = sb("cp_sb", [128, len(CONSTS)], F32)
    for i, v in enumerate(CONSTS):
        nc.const_aps.register(F32, float(v), cp_sb[:, i:i + 1])

    ones_rf = sb("ones_rf", [1, 128], F32)
    ones_cb = sb("ones_cb", [128, 1], BF16)
    ones_cf = sb("ones_cf", [128, 1], F32)
    eye_sb = sb("eye_sb", [128, 128], BF16)
    eyef_sb = sb("eyef_sb", [128, 128], F32)
    tri_sb = sb("tri_sb", [128, 4 * TB], BF16)
    gain_sb = sb("gain_sb", [1, H1], F32)
    vec_sb = {nm: sb(nm + "_s", [128, FC], F32) for nm in vecs}
    wg_sb = sb("wg_sb", [128, FC * E], BF16)
    be1_sb = sb("be1_sb", [128, E * IC], F32)
    qa_sb = sb("qa_sb", [128, FC * RANK], BF16); qb_sb = sb("qb_sb", [RANK, D], BF16)
    ka_sb = sb("ka_sb", [128, FC * RANK], BF16); kb_sb = sb("kb_sb", [RANK, KV_DIM], BF16)
    va_sb = sb("va_sb", [128, FC * RANK], BF16); vb_sb = sb("vb_sb", [RANK, KV_DIM], BF16)
    pa_sb = sb("pa_sb", [128, FC * RANK], BF16); pb_sb = sb("pb_sb", [RANK, D], BF16)
    bB1_sb = sb("bB1_sb", [128, OWNT], F32)
    bA2_sb = sb("bA2_sb", [128, OWNT], F32)
    bB2_sb = sb("bB2_sb", [128, OWNT], F32)
    rowscr = sb("rowscr", [1, 8 * TB], F32)
    ropescr = sb("ropescr", [64, 3 * TB], F32)

    if "psum" not in _KSKIP:
        pA = psl("pA", [128, 512], F32)
        pB = psl("pB", [128, 512], F32)
        pLO = psl("pLO", [128, 512], F32)
        pHI = psl("pHI", [128, 512], F32)
        pC = psl("pC", [128, 512], F32)
        pTb = psl("pTb", [128, 128], BF16)
        pTf = psl("pTf", [128, 128], F32)
        pD = psl("pD", [128, 512], F32)
    else:
        pA = pB = pLO = pHI = pC = pTb = pTf = pD = None

    _bufs = {}

    def nb(name):
        b = Buf(name)
        _bufs[name] = b
        return b

    def npb(name):
        b = Buf(name, is_psum=True)
        _bufs[name] = b
        return b

    b_pA, b_pB, b_pLO, b_pHI, b_pC, b_pTb, b_pTf = (
        npb("pA"), npb("pB"), npb("pLO"), npb("pHI"), npb("pC"), npb("pTb"),
        npb("pTf"))
    b_pD0, b_pD1 = npb("pD0"), npb("pD1")
    b_const = Buf("const", noww=True)
    _bufs["const"] = b_const
    b_rowscr = nb("rowscr")
    b_ropescr = nb("ropescr")

    if "sync0" not in _KSKIP:
        for dst, src in ((cp_sb[:], cp_h.ap()), (eye_sb[:], eye_h.ap()),
                         (eyef_sb[:], eyef_h.ap()), (tri_sb[:], tri_h.ap()),
                         (gain_sb[:], gain), (be1_sb[:], be1),
                         (bB1_sb[:], biasB1), (bA2_sb[:], biasA2), (bB2_sb[:], biasB2)):
            p.dma("sync", dst, src, writes=[b_const])
    if "vecs0" not in _KSKIP:
        for nm in vecs:
            p.dma("sync", vec_sb[nm][:], vecs[nm], writes=[b_const])
    if "cast0" not in _KSKIP:
        for dst, src in ((wg_sb[:], wg),
                         (qa_sb[:], qa), (qb_sb[:], qb), (ka_sb[:], ka),
                         (kb_sb[:], kb), (va_sb[:], va), (vb_sb[:], vb),
                         (pa_sb[:], pa), (pb_sb[:], pb)):
            p.dma("gpsimd", dst, src, writes=[b_const])
    if "memset0" not in _KSKIP:
        p.op("vector", lambda e: e.memset(ones_rf[:], 1.0), writes=[b_const])
        p.op("vector", lambda e: e.memset(ones_cb[:], 1.0), writes=[b_const])
        p.op("vector", lambda e: e.memset(ones_cf[:], 1.0), writes=[b_const])
    p.barrier()

    def rope_emit(src_buf, src, dst, dst_buf, cS, sS, half, sign_lo):
        s2 = ropescr[0:half, 0:TB]
        ta = ropescr[0:half, TB:2 * TB]
        tb2 = ropescr[0:half, 2 * TB:3 * TB]
        r = [src_buf, b_const]
        p.op("gpsimd", lambda e: e.tensor_copy(s2, src[half:2 * half, :]), reads=r, writes=[b_ropescr])
        p.op("gpsimd", lambda e: e.tensor_tensor(ta, src[0:half, :], cS, ALU.mult), reads=r, writes=[b_ropescr])
        p.op("gpsimd", lambda e: e.tensor_tensor(tb2, s2, sS, ALU.mult), reads=[b_ropescr, b_const], writes=[b_ropescr])
        if sign_lo > 0:
            p.op("gpsimd", lambda e: e.tensor_tensor(dst[0:half, :], ta, tb2, ALU.add), reads=[b_ropescr], writes=[dst_buf])
        else:
            p.op("gpsimd", lambda e: e.tensor_tensor(dst[0:half, :], ta, tb2, ALU.subtract), reads=[b_ropescr], writes=[dst_buf])
        p.op("gpsimd", lambda e: e.tensor_tensor(ta, src[0:half, :], sS, ALU.mult), reads=r, writes=[b_ropescr])
        p.op("gpsimd", lambda e: e.tensor_tensor(tb2, s2, cS, ALU.mult), reads=[b_ropescr, b_const], writes=[b_ropescr])
        if sign_lo > 0:
            p.op("gpsimd", lambda e: e.tensor_tensor(dst[half:2 * half, :], tb2, ta, ALU.subtract), reads=[b_ropescr], writes=[dst_buf])
        else:
            p.op("gpsimd", lambda e: e.tensor_tensor(dst[half:2 * half, :], tb2, ta, ALU.add), reads=[b_ropescr], writes=[dst_buf])
        if 2 * half < 128:
            p.op("gpsimd", lambda e: e.tensor_copy(dst[2 * half:128, :], src[2 * half:128, :]),
                 reads=r, writes=[dst_buf])

    ctx1 = ExitStack()
    cos1_sb = sb("cos1_sb", [32, T], F32, ctx1)
    sin1_sb = sb("sin1_sb", [32, T], F32, ctx1)
    if "cosdma" not in _KSKIP:
        p.dma("sync", cos1_sb[:], cos1, writes=[b_const])
        p.dma("sync", sin1_sb[:], sin1, writes=[b_const])
    xt = sb("xt", [128, FC * TB], F32, ctx1)
    b_xt = [nb(f"xt{fc}") for fc in range(FC)]
    xb2 = sb("xb2", [128, 2 * TB], BF16, ctx1)
    sq2 = sb("sq2", [128, 2 * TB], BF16, ctx1)
    b_xb = [nb("xb0"), nb("xb1")]
    b_sq = [nb("sq0"), nb("sq1")]
    lnx = sb("lnx", [128, FC * TB], BF16, ctx1)
    b_lnx = nb("lnx")
    kvw_sb = sb("kvw_sb", [128, 8 * D], BF16, ctx1)
    b_kvw = nb("kvw")
    wslab = sb("wslab", [128, 2 * D], BF16, ctx1)
    b_wslab = [nb("wsl0"), nb("wsl1")]
    t1k = sb("t1k", [RANK, TB], BF16, ctx1); b_t1k = nb("t1k")
    t1v = sb("t1v", [RANK, TB], BF16, ctx1); b_t1v = nb("t1v")
    t1q = sb("t1q", [RANK, TB], BF16, ctx1); b_t1q = nb("t1q")
    kf = sb("kf", [128, TB], F32, ctx1); b_kf = nb("kf")
    lntmp = sb("lntmp", [128, TB], F32, ctx1); b_lntmp = nb("lntmp")
    sqk = sb("sqk", [128, TB], BF16, ctx1); b_sqk = nb("sqk")
    kd = sb("kd", [128, TB], BF16, ctx1); b_kd = nb("kd")
    vstage = sb("vstage", [128, 4 * 129], BF16, ctx1); b_vstage = nb("vstage")
    vnsc = sb("vnsc", [128, TB], F32, ctx1); b_vnsc = nb("vnsc")

    R_MU = rowscr[0:1, 0:TB]
    R_T = rowscr[0:1, TB:2 * TB]
    R_VAR = rowscr[0:1, 2 * TB:3 * TB]
    R_RS = rowscr[0:1, 3 * TB:4 * TB]
    R_MR = rowscr[0:1, 4 * TB:5 * TB]
    R_L = rowscr[0:1, 5 * TB:6 * TB]
    R_R = rowscr[0:1, 6 * TB:7 * TB]
    R_X = rowscr[0:1, 7 * TB:8 * TB]

    def rms_rows(eps, scale_, exp_scale):
        p.op("scalar", lambda e: e.activation(R_L, pD[0:1, :], AF.Ln, bias=eps, scale=scale_),
             reads=[b_pD0], writes=[b_rowscr])
        p.op("scalar", lambda e, es=exp_scale: e.activation(R_R, R_L, AF.Exp, bias=0.0, scale=es),
             reads=[b_rowscr], writes=[b_rowscr])

    def proj_epilogue_qk(oc, tcols, dst_dram, dram_buf, gain_col):
        p.op("vector", lambda e: e.tensor_copy(kf[:], pA[:]), reads=[b_pA], writes=[b_kf])
        p.op("vector", lambda e: e.tensor_tensor(sqk[:], kf[:], kf[:], ALU.mult),
             reads=[b_kf], writes=[b_sqk])
        p.op("tensor", lambda e: e.matmul(pD[0:1, :], ones_cb[:], sqk[:], start=True, stop=True),
             reads=[b_sqk, b_const], writes=[b_pD0])
        rms_rows(EPS_RMS, 1.0 / HD, -0.5)
        if gain_col is not None:
            p.op("vector", lambda e, g=gain_col: e.tensor_scalar(R_R, R_R, g, None, ALU.mult),
                 reads=[b_rowscr, b_const], writes=[b_rowscr])
        p.op("tensor", lambda e: e.matmul(pB[:], ones_rf[:], R_R, start=True, stop=True),
             reads=[b_rowscr, b_const], writes=[b_pB])
        p.op("vector", lambda e: e.tensor_tensor(kf[:], kf[:], pB[:], ALU.mult),
             reads=[b_pB], writes=[b_kf])
        rope_emit(b_kf, kf, kd, b_kd, cos1_sb[:, tcols], sin1_sb[:, tcols], 32, +1)
        p.dma("sync", dst_dram, kd[:], reads=[b_kd], writes=[dram_buf])

    def emit_p1():
        for oc in range(KVH1):
            p.dma("gpsimd", kvw_sb[:, oc * D:(oc + 1) * D], wk1[oc], writes=[b_kvw])
            p.dma("gpsimd", kvw_sb[:, (KVH1 + oc) * D:(KVH1 + oc + 1) * D], wv1[oc],
                  writes=[b_kvw])
        p.op("vector", lambda e: e.memset(vstage[:], 1.0), writes=[b_vstage])
        xr = xfm.rearrange("(fc p) T -> p fc T", p=128)
        for t in range(NT):
            tcols = slice(t * TB, (t + 1) * TB)
            for fc in range(FC):
                p.dma("sync", xt[:, fc * TB:(fc + 1) * TB],
                      xr[:, fc, tcols], writes=[b_xt[fc]])
            for fc in range(FC):
                sl = fc % 2
                xbv = xb2[:, sl * TB:(sl + 1) * TB]
                sqv = sq2[:, sl * TB:(sl + 1) * TB]
                xtv = xt[:, fc * TB:(fc + 1) * TB]
                p.op("vector", lambda e, o=xbv, i=xtv: e.tensor_copy(o, i),
                     reads=[b_xt[fc]], writes=[b_xb[sl]])
                p.op("vector", lambda e, o=sqv, i=xbv: e.tensor_tensor(o, i, i, ALU.mult),
                     reads=[b_xb[sl]], writes=[b_sq[sl]])
                p.op("tensor", lambda e, r=xbv, s=(fc == 0), st=(fc == FC - 1):
                     e.matmul(pD[0:1, :], ones_cb[:], r, start=s, stop=st),
                     reads=[b_xb[sl], b_const], writes=[b_pD0])
                p.op("tensor", lambda e, r=sqv, s=(fc == 0), st=(fc == FC - 1):
                     e.matmul(pD[32:33, :], ones_cb[:], r, start=s, stop=st),
                     reads=[b_sq[sl], b_const], writes=[b_pD1])
            p.op("vector", lambda e: e.tensor_scalar(R_MU, pD[0:1, :], 1.0 / D, None, ALU.mult),
                 reads=[b_pD0], writes=[b_rowscr])
            p.op("vector", lambda e: e.tensor_scalar(R_T, pD[32:33, :], 1.0 / D, None, ALU.mult),
                 reads=[b_pD1], writes=[b_rowscr])
            p.op("vector", lambda e: e.tensor_tensor(R_VAR, R_MU, R_MU, ALU.mult),
                 reads=[b_rowscr], writes=[b_rowscr])
            p.op("vector", lambda e: e.tensor_tensor(R_VAR, R_T, R_VAR, ALU.subtract),
                 reads=[b_rowscr], writes=[b_rowscr])
            p.op("scalar", lambda e: e.activation(R_T, R_VAR, AF.Ln, bias=EPS_LN, scale=1.0),
                 reads=[b_rowscr], writes=[b_rowscr])
            p.op("scalar", lambda e: e.activation(R_RS, R_T, AF.Exp, bias=0.0, scale=-0.5),
                 reads=[b_rowscr], writes=[b_rowscr])
            p.op("vector", lambda e: e.tensor_tensor(R_MR, R_MU, R_RS, ALU.mult),
                 reads=[b_rowscr], writes=[b_rowscr])
            p.op("tensor", lambda e: e.matmul(pA[:], ones_rf[:], R_RS, start=True, stop=True),
                 reads=[b_rowscr, b_const], writes=[b_pA])
            p.op("tensor", lambda e: e.matmul(pB[:], ones_rf[:], R_MR, start=True, stop=True),
                 reads=[b_rowscr, b_const], writes=[b_pB])
            for fc in range(FC):
                xtv = xt[:, fc * TB:(fc + 1) * TB]
                p.op("vector", lambda e, i=xtv: e.tensor_tensor(lntmp[:], i, pA[:], ALU.mult),
                     reads=[b_xt[fc], b_pA], writes=[b_lntmp])
                p.op("vector", lambda e, o=lnx[:, fc * TB:(fc + 1) * TB]:
                     e.tensor_tensor(o, lntmp[:], pB[:], ALU.subtract),
                     reads=[b_lntmp, b_pB], writes=[b_lnx])
            lr = [(ka_sb, t1k, b_t1k), (va_sb, t1v, b_t1v)]
            if t < OWNT:
                lr.append((qa_sb, t1q, b_t1q))
            for A_s, t1b, t1bf in lr:
                for fc in range(FC):
                    p.op("tensor", lambda e, l=A_s[:, fc * RANK:(fc + 1) * RANK],
                         r=lnx[:, fc * TB:(fc + 1) * TB],
                         s=(fc == 0), st=(fc == FC - 1):
                         e.matmul(pC[0:RANK, :], l, r, start=s, stop=st),
                         reads=[b_lnx, b_const], writes=[b_pC])
                p.op("vector", lambda e, o=t1b[:]: e.tensor_copy(o, pC[0:RANK, :]),
                     reads=[b_pC], writes=[t1bf])
            for kind in ("k", "v"):
                for oc in range(KVH1):
                    wsl = oc if kind == "k" else KVH1 + oc
                    Bsb = kb_sb if kind == "k" else vb_sb
                    t1b, t1bf = (t1k, b_t1k) if kind == "k" else (t1v, b_t1v)
                    for fc in range(FC):
                        p.op("tensor", lambda e,
                             l=kvw_sb[:, wsl * D + fc * 128: wsl * D + (fc + 1) * 128],
                             r=lnx[:, fc * TB:(fc + 1) * TB], s=(fc == 0):
                             e.matmul(pA[:], l, r, start=s, stop=False),
                             reads=[b_lnx, b_kvw], writes=[b_pA])
                    p.op("tensor", lambda e, l=Bsb[:, oc * 128:(oc + 1) * 128], r=t1b[:]:
                         e.matmul(pA[:], l, r, start=False, stop=True),
                         reads=[t1bf, b_const], writes=[b_pA])
                    if kind == "k":
                        proj_epilogue_qk(oc, tcols, k1p[oc, :, tcols], DB["k1p"], None)
                    else:
                        p.op("vector", lambda e: e.tensor_copy(kf[:], pA[:]),
                             reads=[b_pA], writes=[b_kf])
                        if t < OWNT:
                            p.op("vector", lambda e: e.tensor_tensor(sqk[:], kf[:], kf[:], ALU.mult),
                                 reads=[b_kf], writes=[b_sqk])
                            p.op("tensor", lambda e: e.matmul(pD[0:1, :], ones_cb[:], sqk[:],
                                                              start=True, stop=True),
                                 reads=[b_sqk, b_const], writes=[b_pD0])
                            rms_rows(1e-30, 1.0, 0.5)
                            p.op("vector", lambda e: e.tensor_scalar(R_R, R_R, 1e-12, None, ALU.max),
                                 reads=[b_rowscr], writes=[b_rowscr])
                            p.op("vector", lambda e: e.reciprocal(R_R, R_R),
                                 reads=[b_rowscr], writes=[b_rowscr])
                            p.op("tensor", lambda e: e.matmul(pB[:], ones_rf[:], R_R, start=True, stop=True),
                                 reads=[b_rowscr, b_const], writes=[b_pB])
                            p.op("vector", lambda e: e.tensor_tensor(vnsc[:], kf[:], pB[:], ALU.mult),
                                 reads=[b_kf, b_pB], writes=[b_vnsc])
                            p.dma("sync", vnp[oc, :, tcols], vnsc[:],
                                  reads=[b_vnsc], writes=[DB["vnp"]])
                        p.op("vector", lambda e: e.tensor_copy(kd[:], kf[:]),
                             reads=[b_kf], writes=[b_kd])
                        for kc in range(4):
                            p.op("tensor", lambda e, i=kd[:, kc * 128:(kc + 1) * 128]:
                                 e.transpose(pTb[:], i, eye_sb[:]),
                                 reads=[b_kd, b_const], writes=[b_pTb])
                            p.op("vector", lambda e, o=vstage[:, kc * 129:kc * 129 + 128]:
                                 e.tensor_copy(o, pTb[:]),
                                 reads=[b_pTb], writes=[b_vstage])
                        base = (oc * NT + t) * 512
                        p.dma("sync",
                              v1tp[base:base + 512, :].rearrange("(kc p) c -> p kc c", p=128),
                              vstage[:].rearrange("p (kc c) -> p kc c", kc=4),
                              reads=[b_vstage], writes=[DB["v1tp"]])
            if t < OWNT:
                for oc in range(H1):
                    wsl = oc % 2
                    p.dma("gpsimd", wslab[:, wsl * D:(wsl + 1) * D], wq1[oc],
                          writes=[b_wslab[wsl]])
                    for fc in range(FC):
                        p.op("tensor", lambda e,
                             l=wslab[:, wsl * D + fc * 128: wsl * D + (fc + 1) * 128],
                             r=lnx[:, fc * TB:(fc + 1) * TB], s=(fc == 0):
                             e.matmul(pA[:], l, r, start=s, stop=False),
                             reads=[b_lnx, b_wslab[wsl]], writes=[b_pA])
                    p.op("tensor", lambda e, l=qb_sb[:, oc * 128:(oc + 1) * 128], r=t1q[:]:
                         e.matmul(pA[:], l, r, start=False, stop=True),
                         reads=[b_t1q, b_const], writes=[b_pA])
                    proj_epilogue_qk(oc, tcols, q1p[oc, :, tcols], DB["q1p"],
                                     gain_sb[0:1, oc:oc + 1])

    if phase_limit >= 1:
        emit_p1()
    p.barrier()
    if "noclose" not in _KSKIP:
        ctx1.close()

    def emit_p23():
        ctx2 = ExitStack()
        Kb = sb("Kb", [128, 8 * TB], BF16, ctx2)
        b_Ks = [nb(f"Ks{s}") for s in range(8)]
        Vtb = sb("Vtb", [128, 8 * 4 * 129], BF16, ctx2)
        b_Vs = [nb(f"Vs{s}") for s in range(8)]
        vns = sb("vns", [128, TB], F32, ctx2); b_vns = nb("vns2")
        qs2 = sb("qs2", [128, 2 * TB], BF16, ctx2)
        b_qs = [nb("qsl0"), nb("qsl1")]
        Pb = sb("Pb", [128, 3 * TB], BF16, ctx2)
        b_P = [nb(f"P{j}") for j in range(3)]
        ysb = sb("ysb", [128, TB], F32, ctx2); b_ysb = nb("ysb")
        tt2 = sb("tt2", [128, TB], F32, ctx2); b_tt2 = nb("tt2")
        y_acc = sb("y_acc", [128, H1 * TOWN], BF16, ctx2); b_yacc = nb("yacc")
        xr2 = xfm.rearrange("(fc p) T -> p fc T", p=128)
        pcnt = [0]
        for kvh in range(KVH1):
            for i in range(OWNT):
                for pos in (i, 4 + i):
                    p.dma("sync", Kb[:, pos * TB:(pos + 1) * TB],
                          k1p[kvh, :, pos * TB:(pos + 1) * TB],
                          reads=[DB["k1p"]], writes=[b_Ks[pos]])
                    vb_ = (kvh * NT + pos) * 512
                    p.dma("sync",
                          Vtb[:, pos * 516:(pos + 1) * 516].rearrange("p (kc c) -> p kc c", kc=4),
                          v1tp[vb_:vb_ + 512, :].rearrange("(kc p) c -> p kc c", p=128),
                          reads=[DB["v1tp"]], writes=[b_Vs[pos]])
                p.dma("sync", vns[:], vnp[kvh, :, i * TB:(i + 1) * TB],
                      reads=[DB["vnp"]], writes=[b_vns])
                POS = list(range(0, i + 1)) + list(range(4, 5 + i))
                for g in range(G1):
                    h = kvh * G1 + g
                    qsl = h % 2
                    qv = qs2[:, qsl * TB:(qsl + 1) * TB]
                    p.dma("sync", qv, q1p[h, :, i * TB:(i + 1) * TB],
                          reads=[DB["q1p"]], writes=[b_qs[qsl]])
                    first = True
                    for idx, pos in enumerate(POS):
                        for kc in range(4):
                            sp_, b_sp = ((pA, b_pA) if (idx * 4 + kc) % 2 == 0
                                         else (pB, b_pB))
                            p.op("tensor", lambda e, o=sp_,
                                 l=Kb[:, pos * TB + kc * 128: pos * TB + (kc + 1) * 128],
                                 r=qv: e.matmul(o[:], l, r, start=True, stop=True),
                                 reads=[b_Ks[pos], b_qs[qsl]], writes=[b_sp])
                            pj = pcnt[0] % 3
                            pcnt[0] += 1
                            Pv = Pb[:, pj * TB:(pj + 1) * TB]
                            bias = bB1_sb[:, i:i + 1] if pos == 4 + i else -C0A
                            p.op("scalar", lambda e, o=Pv, ii=sp_, bi=bias:
                                 e.activation(o, ii[:], AF.Exp, bias=bi, scale=SCL),
                                 reads=[b_sp, b_const], writes=[b_P[pj]])
                            if pos == i:
                                p.op("vector", lambda e, o=Pv,
                                     m=tri_sb[:, kc * TB:(kc + 1) * TB]:
                                     e.tensor_tensor(o, o, m, ALU.mult),
                                     reads=[b_P[pj], b_const], writes=[b_P[pj]])
                            last = (idx == len(POS) - 1 and kc == 3)
                            vb2 = pos * 516 + kc * 129
                            p.op("tensor", lambda e, l=Vtb[:, vb2:vb2 + 65], r=Pv,
                                 s=first, st=last:
                                 e.matmul(pLO[0:65, :], l, r, start=s, stop=st),
                                 reads=[b_Vs[pos], b_P[pj]], writes=[b_pLO])
                            p.op("tensor", lambda e, l=Vtb[:, vb2 + 64:vb2 + 129], r=Pv,
                                 s=first, st=last:
                                 e.matmul(pHI[0:65, :], l, r, start=s, stop=st),
                                 reads=[b_Vs[pos], b_P[pj]], writes=[b_pHI])
                            first = False
                    p.op("vector", lambda e: e.tensor_copy(ysb[0:64, :], pLO[0:64, :]),
                         reads=[b_pLO], writes=[b_ysb])
                    p.op("vector", lambda e: e.tensor_copy(ysb[64:128, :], pHI[0:64, :]),
                         reads=[b_pHI], writes=[b_ysb])
                    p.op("vector", lambda e: e.tensor_copy(R_X, pHI[64:65, :]),
                         reads=[b_pHI], writes=[b_rowscr])
                    p.op("vector", lambda e: e.reciprocal(R_X, R_X),
                         reads=[b_rowscr], writes=[b_rowscr])
                    p.op("tensor", lambda e: e.matmul(pC[:], ones_rf[:], R_X,
                                                      start=True, stop=True),
                         reads=[b_rowscr, b_const], writes=[b_pC])
                    p.op("vector", lambda e: e.tensor_tensor(tt2[:], ysb[:], vns[:], ALU.mult),
                         reads=[b_ysb, b_vns], writes=[b_tt2])
                    p.op("tensor", lambda e: e.matmul(pD[0:1, :], ones_cf[:], tt2[:],
                                                      start=True, stop=True),
                         reads=[b_tt2, b_const], writes=[b_pD0])
                    p.op("vector", lambda e: e.tensor_copy(R_L, pD[0:1, :]),
                         reads=[b_pD0], writes=[b_rowscr])
                    p.op("tensor", lambda e: e.matmul(pA[:], ones_rf[:], R_L,
                                                      start=True, stop=True),
                         reads=[b_rowscr, b_const], writes=[b_pA])
                    p.op("vector", lambda e: e.tensor_tensor(tt2[:], vns[:], pA[:], ALU.mult),
                         reads=[b_vns, b_pA], writes=[b_tt2])
                    p.op("vector", lambda e: e.tensor_tensor(ysb[:], ysb[:], tt2[:], ALU.subtract),
                         reads=[b_tt2], writes=[b_ysb])
                    p.op("vector", lambda e,
                         o=y_acc[:, h * TOWN + i * TB: h * TOWN + (i + 1) * TB]:
                         e.tensor_tensor(o, ysb[:], pC[:], ALU.mult),
                         reads=[b_ysb, b_pC], writes=[b_yacc])
                    if PARANOID:
                        p.barrier()
        if "y1" in debug:
            o = nc.dram_tensor("dbg_y1", [128, H1 * TOWN], BF16,
                               kind="ExternalOutput").ap()
            p.dma("sync", o, y_acc[:], reads=[b_yacc], writes=[Buf("dbgy1")])
        t1p = sb("t1p", [RANK, OWNT * TB], BF16, ctx2); b_t1p = nb("t1p")
        statrow = sb("statrow", [1, 8 * TB], F32, ctx2); b_stat = nb("statrow")
        wsl2 = sb("wsl2", [128, 2 * D], BF16, ctx2)
        b_wsl2 = [nb("w2a"), nb("w2b")]
        xc2 = sb("xc2", [128, 2 * TB], F32, ctx2)
        b_xc2 = [nb("xca"), nb("xcb")]
        x1sb = sb("x1sb", [128, TB], F32, ctx2); b_x1sb = nb("x1sb")
        x1b6 = sb("x1b6", [128, TB], BF16, ctx2); b_x1b6 = nb("x1b6")
        sq6 = sb("sq6", [128, TB], BF16, ctx2); b_sq6 = nb("sq6")
        for i in range(OWNT):
            for fc in range(FC):
                p.op("tensor", lambda e, l=pa_sb[:, fc * RANK:(fc + 1) * RANK],
                     r=y_acc[:, fc * TOWN + i * TB: fc * TOWN + (i + 1) * TB],
                     s=(fc == 0), st=(fc == FC - 1):
                     e.matmul(pC[0:RANK, :], l, r, start=s, stop=st),
                     reads=[b_yacc, b_const], writes=[b_pC])
            p.op("vector", lambda e, o=t1p[:, i * TB:(i + 1) * TB]:
                 e.tensor_copy(o, pC[0:RANK, :]), reads=[b_pC], writes=[b_t1p])
        for oc in range(FC):
            wsl = oc % 2
            p.dma("gpsimd", wsl2[:, wsl * D:(wsl + 1) * D], wpj[oc],
                  writes=[b_wsl2[wsl]])
            for i in range(OWNT):
                for fc in range(FC):
                    p.op("tensor", lambda e,
                         l=wsl2[:, wsl * D + fc * 128: wsl * D + (fc + 1) * 128],
                         r=y_acc[:, fc * TOWN + i * TB: fc * TOWN + (i + 1) * TB],
                         s=(fc == 0): e.matmul(pA[:], l, r, start=s, stop=False),
                         reads=[b_yacc, b_wsl2[wsl]], writes=[b_pA])
                p.op("tensor", lambda e, l=pb_sb[:, oc * 128:(oc + 1) * 128],
                     r=t1p[:, i * TB:(i + 1) * TB]:
                     e.matmul(pA[:], l, r, start=False, stop=True),
                     reads=[b_t1p, b_const], writes=[b_pA])
                xsl = i % 2
                xv = xc2[:, xsl * TB:(xsl + 1) * TB]
                p.dma("sync", xv, xr2[:, oc, i * TB:(i + 1) * TB], writes=[b_xc2[xsl]])
                p.op("vector", lambda e, x_=xv: e.tensor_tensor(x1sb[:], pA[:], x_, ALU.add),
                     reads=[b_pA, b_xc2[xsl]], writes=[b_x1sb])
                p.dma("sync", x1p[oc, :, i * TB:(i + 1) * TB], x1sb[:],
                      reads=[b_x1sb], writes=[DB["x1p"]])
                p.op("vector", lambda e: e.tensor_copy(x1b6[:], x1sb[:]),
                     reads=[b_x1sb], writes=[b_x1b6])
                p.op("vector", lambda e: e.tensor_tensor(sq6[:], x1b6[:], x1b6[:], ALU.mult),
                     reads=[b_x1b6], writes=[b_sq6])
                p.op("tensor", lambda e: e.matmul(pD[0:1, :], ones_cb[:], x1b6[:],
                                                  start=True, stop=True),
                     reads=[b_x1b6, b_const], writes=[b_pD0])
                p.op("tensor", lambda e: e.matmul(pD[32:33, :], ones_cb[:], sq6[:],
                                                  start=True, stop=True),
                     reads=[b_sq6, b_const], writes=[b_pD1])
                s1v = statrow[0:1, i * TB:(i + 1) * TB]
                s2v = statrow[0:1, (4 + i) * TB:(5 + i) * TB]
                if oc == 0:
                    p.op("vector", lambda e, o=s1v: e.tensor_copy(o, pD[0:1, :]),
                         reads=[b_pD0], writes=[b_stat])
                    p.op("vector", lambda e, o=s2v: e.tensor_copy(o, pD[32:33, :]),
                         reads=[b_pD1], writes=[b_stat])
                else:
                    p.op("vector", lambda e, o=s1v: e.tensor_tensor(o, o, pD[0:1, :], ALU.add),
                         reads=[b_pD0], writes=[b_stat])
                    p.op("vector", lambda e, o=s2v: e.tensor_tensor(o, o, pD[32:33, :], ALU.add),
                         reads=[b_pD1], writes=[b_stat])
                if PARANOID:
                    p.barrier()
        for i in range(OWNT):
            s1v = statrow[0:1, i * TB:(i + 1) * TB]
            s2v = statrow[0:1, (4 + i) * TB:(5 + i) * TB]
            p.op("vector", lambda e, s=s1v: e.tensor_scalar(R_MU, s, 1.0 / D, None, ALU.mult),
                 reads=[b_stat], writes=[b_rowscr])
            p.op("vector", lambda e, s=s2v: e.tensor_scalar(R_T, s, 1.0 / D, None, ALU.mult),
                 reads=[b_stat], writes=[b_rowscr])
            p.op("vector", lambda e: e.tensor_tensor(R_VAR, R_MU, R_MU, ALU.mult),
                 reads=[b_rowscr], writes=[b_rowscr])
            p.op("vector", lambda e: e.tensor_tensor(R_VAR, R_T, R_VAR, ALU.subtract),
                 reads=[b_rowscr], writes=[b_rowscr])
            p.op("scalar", lambda e: e.activation(R_L, R_VAR, AF.Ln, bias=EPS_LN, scale=1.0),
                 reads=[b_rowscr], writes=[b_rowscr])
            p.op("scalar", lambda e: e.activation(R_RS, R_L, AF.Exp, bias=0.0, scale=-0.5),
                 reads=[b_rowscr], writes=[b_rowscr])
            p.dma("sync", rows_d[0:1, i * TB:(i + 1) * TB], R_MU,
                  reads=[b_rowscr], writes=[DB["rows_d"]])
            p.dma("sync", rows_d[1:2, i * TB:(i + 1) * TB], R_RS,
                  reads=[b_rowscr], writes=[DB["rows_d"]])
            p.op("vector", lambda e: e.tensor_scalar(R_T, R_VAR, EPS_LN, None, ALU.add),
                 reads=[b_rowscr], writes=[b_rowscr])
            p.op("vector", lambda e: e.reciprocal(R_T, R_T), reads=[b_rowscr], writes=[b_rowscr])
            p.op("vector", lambda e: e.tensor_tensor(R_T, R_VAR, R_T, ALU.mult),
                 reads=[b_rowscr], writes=[b_rowscr])
            p.op("scalar", lambda e: e.activation(R_T, R_T, AF.Ln, bias=EPS_LN, scale=1.0),
                 reads=[b_rowscr], writes=[b_rowscr])
            p.op("vector", lambda e: e.tensor_tensor(R_T, R_T, R_L, ALU.add),
                 reads=[b_rowscr], writes=[b_rowscr])
            p.op("scalar", lambda e: e.activation(R_X, R_T, AF.Exp, bias=0.0, scale=-0.5),
                 reads=[b_rowscr], writes=[b_rowscr])
            p.op("vector", lambda e: e.tensor_tensor(R_MR, R_MU, R_X, ALU.mult),
                 reads=[b_rowscr], writes=[b_rowscr])
            p.dma("sync", rows_d[2:3, i * TB:(i + 1) * TB], R_X,
                  reads=[b_rowscr], writes=[DB["rows_d"]])
            p.dma("sync", rows_d[3:4, i * TB:(i + 1) * TB], R_MR,
                  reads=[b_rowscr], writes=[DB["rows_d"]])
        ctx2.close()
        p.barrier()
        ctx3 = ExitStack()
        g1_sb = sb("g1_sb", [128, FC * TOWN], BF16, ctx3)
        b_g1 = nb("g1")
        xg = sb("xg", [128, 2 * TB], F32, ctx3)
        b_xg = [nb("xga"), nb("xgb")]
        gt = sb("gt", [128, TB], F32, ctx3); b_gt = nb("gt")
        for i in range(OWNT):
            p.dma("sync", R_L, rows_d[2:3, i * TB:(i + 1) * TB],
                  reads=[DB["rows_d"]], writes=[b_rowscr])
            p.dma("sync", R_MR, rows_d[3:4, i * TB:(i + 1) * TB],
                  reads=[DB["rows_d"]], writes=[b_rowscr])
            p.op("tensor", lambda e: e.matmul(pA[:], ones_rf[:], R_L, start=True, stop=True),
                 reads=[b_rowscr, b_const], writes=[b_pA])
            p.op("tensor", lambda e: e.matmul(pB[:], ones_rf[:], R_MR, start=True, stop=True),
                 reads=[b_rowscr, b_const], writes=[b_pB])
            for fc in range(FC):
                xsl = fc % 2
                xv = xg[:, xsl * TB:(xsl + 1) * TB]
                p.dma("sync", xv, x1p[fc, :, i * TB:(i + 1) * TB],
                      reads=[DB["x1p"]], writes=[b_xg[xsl]])
                p.op("vector", lambda e, x_=xv: e.tensor_tensor(gt[:], x_, pA[:], ALU.mult),
                     reads=[b_xg[xsl], b_pA], writes=[b_gt])
                p.op("vector", lambda e: e.tensor_tensor(gt[:], gt[:], pB[:], ALU.subtract),
                     reads=[b_pB], writes=[b_gt])
                p.op("vector", lambda e,
                     o=g1_sb[:, fc * TOWN + i * TB: fc * TOWN + (i + 1) * TB],
                     w1=vec_sb["l1w"][:, fc:fc + 1], b1=vec_sb["l1b"][:, fc:fc + 1]:
                     e.tensor_scalar(o, gt[:], w1, b1, ALU.mult, ALU.add),
                     reads=[b_gt, b_const], writes=[b_g1])
                if PARANOID:
                    p.barrier()
        if "g1" in debug:
            o = nc.dram_tensor("dbg_g1", [128, FC * TOWN], BF16,
                               kind="ExternalOutput").ap()
            p.dma("sync", o, g1_sb[:], reads=[b_g1], writes=[Buf("dbgg1")])
        return ctx3, g1_sb, b_g1

    def emit_p3b(g1_sb, b_g1):
        c3 = ExitStack()
        cos2_sb = sb("cos2_sb", [64, TOWN], F32, c3)
        sin2_sb = sb("sin2_sb", [64, TOWN], F32, c3)
        b_tab2 = Buf("tab2", noww=True)
        p.dma("sync", cos2_sb[:], cos2, writes=[b_tab2])
        p.dma("sync", sin2_sb[:], sin2, writes=[b_tab2])
        wsl3 = sb("wsl3", [128, 2 * D], BF16, c3)
        b_wsl3 = [nb("w3a"), nb("w3b")]
        pf3 = sb("pf3", [128, TB], F32, c3); b_pf3 = nb("pf3")
        pd3 = sb("pd3", [128, TB], BF16, c3); b_pd3 = nb("pd3")
        vst2 = sb("vst2", [128, 4 * 129], BF16, c3); b_vst2 = nb("vst2")
        p.op("vector", lambda e: e.memset(vst2[:], 1.0), writes=[b_vst2])

        def rope2(tcols):
            cS = cos2_sb[:, tcols]
            sS = sin2_sb[:, tcols]
            s2 = ropescr[0:64, 0:TB]
            ta = ropescr[0:64, TB:2 * TB]
            tb3 = ropescr[0:64, 2 * TB:3 * TB]
            r = [b_pf3, b_tab2]
            p.op("gpsimd", lambda e: e.tensor_copy(s2, pf3[64:128, :]), reads=r, writes=[b_ropescr])
            p.op("gpsimd", lambda e: e.tensor_tensor(ta, pf3[0:64, :], cS, ALU.mult), reads=r, writes=[b_ropescr])
            p.op("gpsimd", lambda e: e.tensor_tensor(tb3, s2, sS, ALU.mult), reads=[b_ropescr, b_tab2], writes=[b_ropescr])
            p.op("gpsimd", lambda e: e.tensor_tensor(pd3[0:64, :], ta, tb3, ALU.subtract), reads=[b_ropescr], writes=[b_pd3])
            p.op("gpsimd", lambda e: e.tensor_tensor(ta, pf3[0:64, :], sS, ALU.mult), reads=r, writes=[b_ropescr])
            p.op("gpsimd", lambda e: e.tensor_tensor(tb3, s2, cS, ALU.mult), reads=[b_ropescr, b_tab2], writes=[b_ropescr])
            p.op("gpsimd", lambda e: e.tensor_tensor(pd3[64:128, :], tb3, ta, ALU.add), reads=[b_ropescr], writes=[b_pd3])

        for kind, wsrc, bnm in (("k", wk2, "bk2"), ("v", wv2, "bv2"), ("q", wq2, "bq2")):
            for oc in range(FC):
                wsl = oc % 2
                p.dma("gpsimd", wsl3[:, wsl * D:(wsl + 1) * D], wsrc[oc],
                      writes=[b_wsl3[wsl]])
                for i in range(OWNT):
                    for fc in range(FC):
                        p.op("tensor", lambda e,
                             l=wsl3[:, wsl * D + fc * 128: wsl * D + (fc + 1) * 128],
                             r=g1_sb[:, fc * TOWN + i * TB: fc * TOWN + (i + 1) * TB],
                             s=(fc == 0), st=(fc == FC - 1):
                             e.matmul(pA[:], l, r, start=s, stop=st),
                             reads=[b_g1, b_wsl3[wsl]], writes=[b_pA])
                    p.op("vector", lambda e, bc=vec_sb[bnm][:, oc:oc + 1]:
                         e.tensor_scalar(pf3[:], pA[:], bc, None, ALU.add),
                         reads=[b_pA, b_const], writes=[b_pf3])
                    tcols = slice(i * TB, (i + 1) * TB)
                    if kind == "q":
                        rope2(tcols)
                        p.dma("sync", q2p[oc, :, tcols], pd3[:],
                              reads=[b_pd3], writes=[DB["q2p"]])
                    elif kind == "k":
                        rope2(tcols)
                        rb = oc * 512 + i * 128
                        p.dma("sync", k2src[rb:rb + 128, :], pd3[:],
                              reads=[b_pd3], writes=[DB["k2src"]])
                    else:
                        p.op("vector", lambda e: e.tensor_copy(pd3[:], pf3[:]),
                             reads=[b_pf3], writes=[b_pd3])
                        for kc in range(4):
                            p.op("tensor", lambda e, ii=pd3[:, kc * 128:(kc + 1) * 128]:
                                 e.transpose(pTb[:], ii, eye_sb[:]),
                                 reads=[b_pd3, b_const], writes=[b_pTb])
                            p.op("vector", lambda e, o=vst2[:, kc * 129:kc * 129 + 128]:
                                 e.tensor_copy(o, pTb[:]),
                                 reads=[b_pTb], writes=[b_vst2])
                        rb = ((oc * OWNT + i) * 4) * 128
                        p.dma("sync",
                              v2src[rb:rb + 512, :].rearrange("(kc p) c -> p kc c", p=128),
                              vst2[:].rearrange("p (kc c) -> p kc c", kc=4),
                              reads=[b_vst2], writes=[DB["v2src"]])
            if kind == "k":
                p.cc(lambda e: e.collective_compute(
                    "AllGather", ALU.bypass,
                    replica_groups=[[0, 1], [2, 3], [4, 5], [6, 7]],
                    ins=[k2src.opt()], outs=[k2g.opt()]),
                    reads=[DB["k2src"]], writes=[DB["k2g"]])
            elif kind == "v":
                p.cc(lambda e: e.collective_compute(
                    "AllGather", ALU.bypass,
                    replica_groups=[[0, 1], [2, 3], [4, 5], [6, 7]],
                    ins=[v2src.opt()], outs=[v2g.opt()]),
                    reads=[DB["v2src"]], writes=[DB["v2g"]])
        c3.close()

    def emit_p4():
        c4 = ExitStack()
        mat_sb = sb("mat_sb", [128, 4 * TB], BF16, c4)
        mbt_sb = sb("mbt_sb", [128, 4 * TB], BF16, c4)
        b_msk = Buf("msk", noww=True)
        p.dma("sync", mat_sb[:], mat_in, writes=[b_msk])
        p.dma("sync", mbt_sb[:], mbt_in, writes=[b_msk])
        K2b = sb("K2b", [128, 8 * TB], BF16, c4)
        b_K2 = [nb(f"K2s{s_}") for s_ in range(8)]
        V2tb = sb("V2tb", [128, 8 * 516], BF16, c4)
        b_V2 = [nb(f"V2s{s_}") for s_ in range(8)]
        q4 = sb("q4", [128, 2 * TB], BF16, c4)
        b_q4 = [nb("q4a"), nb("q4b")]
        Pb4 = sb("Pb4", [128, 3 * TB], BF16, c4)
        b_P4 = [nb(f"P4{j}") for j in range(3)]
        y4 = sb("y4", [128, TB], F32, c4); b_y4 = nb("y4")
        y2_acc = sb("y2_acc", [128, H1 * TOWN], BF16, c4)
        b_y2acc = Buf("y2acc", noww=True)
        pcnt4 = [0]
        for h in range(H1):
            for i in range(OWNT):
                for pos in (i, 4 + i):
                    rb = (pos // 4) * (H1 * 512) + h * 512 + (pos % 4) * 128
                    p.dma("sync", K2b[:, pos * TB:(pos + 1) * TB],
                          k2g[rb:rb + 128, :], reads=[DB["k2g"]], writes=[b_K2[pos]])
                    vb_ = (pos // 4) * (H1 * OWNT * 4 * 128) + ((h * OWNT + pos % 4) * 4) * 128
                    p.dma("sync",
                          V2tb[:, pos * 516:(pos + 1) * 516].rearrange("p (kc c) -> p kc c", kc=4),
                          v2g[vb_:vb_ + 512, :].rearrange("(kc p) c -> p kc c", p=128),
                          reads=[DB["v2g"]], writes=[b_V2[pos]])
                qsl = i % 2
                qv = q4[:, qsl * TB:(qsl + 1) * TB]
                p.dma("sync", qv, q2p[h, :, i * TB:(i + 1) * TB],
                      reads=[DB["q2p"]], writes=[b_q4[qsl]])
                POS = list(range(0, i + 1)) + list(range(4, 5 + i))
                first = True
                for idx, pos in enumerate(POS):
                    for kc in range(4):
                        sp_, b_sp = ((pA, b_pA) if (idx * 4 + kc) % 2 == 0
                                     else (pB, b_pB))
                        p.op("tensor", lambda e, o=sp_,
                             l=K2b[:, pos * TB + kc * 128: pos * TB + (kc + 1) * 128],
                             r=qv: e.matmul(o[:], l, r, start=True, stop=True),
                             reads=[b_K2[pos], b_q4[qsl]], writes=[b_sp])
                        pj = pcnt4[0] % 3
                        pcnt4[0] += 1
                        Pv = Pb4[:, pj * TB:(pj + 1) * TB]
                        if pos == i:
                            bias = bA2_sb[:, i:i + 1]
                        elif pos == 4 + i:
                            bias = bB2_sb[:, i:i + 1]
                        else:
                            bias = -C0B
                        p.op("scalar", lambda e, o=Pv, ii=sp_, bi=bias:
                             e.activation(o, ii[:], AF.Exp, bias=bi, scale=SCL),
                             reads=[b_sp, b_const], writes=[b_P4[pj]])
                        if pos == i or pos == 4 + i:
                            msk = mat_sb if pos == i else mbt_sb
                            meng = "vector" if (h + i) % 2 == 0 else "gpsimd"
                            p.op(meng, lambda e, o=Pv, m=msk[:, kc * TB:(kc + 1) * TB]:
                                 e.tensor_tensor(o, o, m, ALU.mult),
                                 reads=[b_P4[pj], b_msk], writes=[b_P4[pj]])
                        last = (idx == len(POS) - 1 and kc == 3)
                        vb2 = pos * 516 + kc * 129
                        p.op("tensor", lambda e, l=V2tb[:, vb2:vb2 + 65], r=Pv,
                             s=first, st=last:
                             e.matmul(pLO[0:65, :], l, r, start=s, stop=st),
                             reads=[b_V2[pos], b_P4[pj]], writes=[b_pLO])
                        p.op("tensor", lambda e, l=V2tb[:, vb2 + 64:vb2 + 129], r=Pv,
                             s=first, st=last:
                             e.matmul(pHI[0:65, :], l, r, start=s, stop=st),
                             reads=[b_V2[pos], b_P4[pj]], writes=[b_pHI])
                        first = False
                p.op("vector", lambda e: e.tensor_copy(y4[0:64, :], pLO[0:64, :]),
                     reads=[b_pLO], writes=[b_y4])
                p.op("vector", lambda e: e.tensor_copy(y4[64:128, :], pHI[0:64, :]),
                     reads=[b_pHI], writes=[b_y4])
                p.op("vector", lambda e: e.tensor_copy(R_X, pHI[64:65, :]),
                     reads=[b_pHI], writes=[b_rowscr])
                p.op("vector", lambda e: e.reciprocal(R_X, R_X),
                     reads=[b_rowscr], writes=[b_rowscr])
                p.op("tensor", lambda e: e.matmul(pC[:], ones_rf[:], R_X,
                                                  start=True, stop=True),
                     reads=[b_rowscr, b_const], writes=[b_pC])
                p.op("vector", lambda e,
                     o=y2_acc[:, h * TOWN + i * TB: h * TOWN + (i + 1) * TB]:
                     e.tensor_tensor(o, y4[:], pC[:], ALU.mult),
                     reads=[b_y4, b_pC], writes=[b_y2acc])
        return c4, y2_acc, b_y2acc

    def emit_p56(y2_acc, b_y2acc):
        c5 = ExitStack()
        wsl5 = sb("wsl5", [128, 2 * D], BF16, c5)
        b_wsl5 = [nb("w5a"), nb("w5b")]
        stat5 = sb("stat5", [1, 8 * TB], F32, c5); b_st5 = nb("stat5")
        mubc = sb("mubc", [128, OWNT * TB], F32, c5); b_mubc = Buf("mubc", noww=True)
        rsbc = sb("rsbc", [128, OWNT * TB], F32, c5); b_rsbc = Buf("rsbc", noww=True)
        xc5 = sb("xc5", [128, 2 * TB], F32, c5)
        b_xc5 = [nb("x5a"), nb("x5b")]
        asb = sb("asb", [128, TB], F32, c5); b_asb = nb("asb")
        hsb = sb("hsb", [128, TB], F32, c5); b_hsb = nb("hsb")
        hb5 = sb("hb5", [128, TB], BF16, c5); b_hb5 = nb("hb5")
        sq5 = sb("sq5", [128, TB], BF16, c5); b_sq5 = nb("sq5")
        for i in range(OWNT):
            p.dma("sync", R_L, rows_d[0:1, i * TB:(i + 1) * TB],
                  reads=[DB["rows_d"]], writes=[b_rowscr])
            p.dma("sync", R_MR, rows_d[1:2, i * TB:(i + 1) * TB],
                  reads=[DB["rows_d"]], writes=[b_rowscr])
            p.op("tensor", lambda e: e.matmul(pA[:], ones_rf[:], R_L, start=True, stop=True),
                 reads=[b_rowscr, b_const], writes=[b_pA])
            p.op("tensor", lambda e: e.matmul(pB[:], ones_rf[:], R_MR, start=True, stop=True),
                 reads=[b_rowscr, b_const], writes=[b_pB])
            p.op("vector", lambda e, o=mubc[:, i * TB:(i + 1) * TB]: e.tensor_copy(o, pA[:]),
                 reads=[b_pA], writes=[b_mubc])
            p.op("vector", lambda e, o=rsbc[:, i * TB:(i + 1) * TB]: e.tensor_copy(o, pB[:]),
                 reads=[b_pB], writes=[b_rsbc])
        for oc in range(FC):
            wsl = oc % 2
            p.dma("gpsimd", wsl5[:, wsl * D:(wsl + 1) * D], wo2[oc],
                  writes=[b_wsl5[wsl]])
            for i in range(OWNT):
                for fc in range(FC):
                    p.op("tensor", lambda e,
                         l=wsl5[:, wsl * D + fc * 128: wsl * D + (fc + 1) * 128],
                         r=y2_acc[:, fc * TOWN + i * TB: fc * TOWN + (i + 1) * TB],
                         s=(fc == 0), st=(fc == FC - 1):
                         e.matmul(pC[:], l, r, start=s, stop=st),
                         reads=[b_y2acc, b_wsl5[wsl]], writes=[b_pC])
                p.op("vector", lambda e, bc=vec_sb["bo2"][:, oc:oc + 1]:
                     e.tensor_scalar(asb[:], pC[:], bc, None, ALU.add),
                     reads=[b_pC, b_const], writes=[b_asb])
                xsl = i % 2
                xv = xc5[:, xsl * TB:(xsl + 1) * TB]
                p.dma("sync", xv, x1p[oc, :, i * TB:(i + 1) * TB],
                      reads=[DB["x1p"]], writes=[b_xc5[xsl]])
                p.op("vector", lambda e, x_=xv, m=mubc[:, i * TB:(i + 1) * TB]:
                     e.tensor_tensor(hsb[:], x_, m, ALU.subtract),
                     reads=[b_xc5[xsl], b_mubc], writes=[b_hsb])
                p.op("vector", lambda e, rr=rsbc[:, i * TB:(i + 1) * TB]:
                     e.tensor_tensor(hsb[:], hsb[:], rr, ALU.mult),
                     reads=[b_rsbc], writes=[b_hsb])
                p.op("vector", lambda e: e.tensor_tensor(hsb[:], hsb[:], asb[:], ALU.add),
                     reads=[b_asb], writes=[b_hsb])
                p.dma("sync", hp[oc, :, i * TB:(i + 1) * TB], hsb[:],
                      reads=[b_hsb], writes=[DB["hp"]])
                p.op("vector", lambda e: e.tensor_copy(hb5[:], hsb[:]),
                     reads=[b_hsb], writes=[b_hb5])
                p.op("vector", lambda e: e.tensor_tensor(sq5[:], hb5[:], hb5[:], ALU.mult),
                     reads=[b_hb5], writes=[b_sq5])
                p.op("tensor", lambda e: e.matmul(pD[0:1, :], ones_cb[:], hb5[:],
                                                  start=True, stop=True),
                     reads=[b_hb5, b_const], writes=[b_pD0])
                p.op("tensor", lambda e: e.matmul(pD[32:33, :], ones_cb[:], sq5[:],
                                                  start=True, stop=True),
                     reads=[b_sq5, b_const], writes=[b_pD1])
                s1v = stat5[0:1, i * TB:(i + 1) * TB]
                s2v = stat5[0:1, (4 + i) * TB:(5 + i) * TB]
                if oc == 0:
                    p.op("vector", lambda e, o=s1v: e.tensor_copy(o, pD[0:1, :]),
                         reads=[b_pD0], writes=[b_st5])
                    p.op("vector", lambda e, o=s2v: e.tensor_copy(o, pD[32:33, :]),
                         reads=[b_pD1], writes=[b_st5])
                else:
                    p.op("vector", lambda e, o=s1v: e.tensor_tensor(o, o, pD[0:1, :], ALU.add),
                         reads=[b_pD0], writes=[b_st5])
                    p.op("vector", lambda e, o=s2v: e.tensor_tensor(o, o, pD[32:33, :], ALU.add),
                         reads=[b_pD1], writes=[b_st5])
        for i in range(OWNT):
            s1v = stat5[0:1, i * TB:(i + 1) * TB]
            s2v = stat5[0:1, (4 + i) * TB:(5 + i) * TB]
            p.op("vector", lambda e, s_=s1v: e.tensor_scalar(R_MU, s_, 1.0 / D, None, ALU.mult),
                 reads=[b_st5], writes=[b_rowscr])
            p.op("vector", lambda e, s_=s2v: e.tensor_scalar(R_T, s_, 1.0 / D, None, ALU.mult),
                 reads=[b_st5], writes=[b_rowscr])
            p.op("vector", lambda e: e.tensor_tensor(R_VAR, R_MU, R_MU, ALU.mult),
                 reads=[b_rowscr], writes=[b_rowscr])
            p.op("vector", lambda e: e.tensor_tensor(R_VAR, R_T, R_VAR, ALU.subtract),
                 reads=[b_rowscr], writes=[b_rowscr])
            p.op("scalar", lambda e: e.activation(R_L, R_VAR, AF.Ln, bias=EPS_LN, scale=1.0),
                 reads=[b_rowscr], writes=[b_rowscr])
            p.op("scalar", lambda e: e.activation(R_RS, R_L, AF.Exp, bias=0.0, scale=-0.5),
                 reads=[b_rowscr], writes=[b_rowscr])
            p.op("vector", lambda e: e.tensor_tensor(R_MR, R_MU, R_RS, ALU.mult),
                 reads=[b_rowscr], writes=[b_rowscr])
            p.dma("sync", rows_d[2:3, i * TB:(i + 1) * TB], R_RS,
                  reads=[b_rowscr], writes=[DB["rows_d"]])
            p.dma("sync", rows_d[3:4, i * TB:(i + 1) * TB], R_MR,
                  reads=[b_rowscr], writes=[DB["rows_d"]])
        gt5 = sb("gt5", [128, TB], F32, c5); b_gt5 = nb("gt5")
        for i in range(OWNT):
            p.dma("sync", R_RS, rows_d[2:3, i * TB:(i + 1) * TB],
                  reads=[DB["rows_d"]], writes=[b_rowscr])
            p.dma("sync", R_MR, rows_d[3:4, i * TB:(i + 1) * TB],
                  reads=[DB["rows_d"]], writes=[b_rowscr])
            p.op("tensor", lambda e: e.matmul(pA[:], ones_rf[:], R_RS, start=True, stop=True),
                 reads=[b_rowscr, b_const], writes=[b_pA])
            p.op("tensor", lambda e: e.matmul(pB[:], ones_rf[:], R_MR, start=True, stop=True),
                 reads=[b_rowscr, b_const], writes=[b_pB])
            for fc in range(FC):
                xsl = fc % 2
                xv = xc5[:, xsl * TB:(xsl + 1) * TB]
                p.dma("sync", xv, hp[fc, :, i * TB:(i + 1) * TB],
                      reads=[DB["hp"]], writes=[b_xc5[xsl]])
                p.op("vector", lambda e, x_=xv: e.tensor_tensor(gt5[:], x_, pA[:], ALU.mult),
                     reads=[b_xc5[xsl], b_pA], writes=[b_gt5])
                p.op("vector", lambda e: e.tensor_tensor(gt5[:], gt5[:], pB[:], ALU.subtract),
                     reads=[b_pB], writes=[b_gt5])
                p.op("vector", lambda e, w1=vec_sb["l2w"][:, fc:fc + 1],
                     b1=vec_sb["l2b"][:, fc:fc + 1]:
                     e.tensor_scalar(hb5[:], gt5[:], w1, b1, ALU.mult, ALU.add),
                     reads=[b_gt5, b_const], writes=[b_hb5])
                p.dma("sync", g2p[fc, :, i * TB:(i + 1) * TB], hb5[:],
                      reads=[b_hb5], writes=[DB["g2p"]])
        c5.close()

    def emit_p7():
        c7 = ExitStack()
        g2s = sb("g2s", [128, FC * TB], BF16, c7)
        b_g2s = [nb(f"g2s{fc}") for fc in range(FC)]
        hmid = sb("hmid", [128, E * IC * TB], BF16, c7)
        b_hmid = nb("hmid")
        w1sl = sb("w1sl", [128, 2 * D], BF16, c7)
        b_w1sl = [nb("w1a"), nb("w1b")]
        w2sl = sb("w2sl", [128, 8 * D], BF16, c7)
        b_w2sl = [nb(f"w2s{j}") for j in range(8)]
        be2s = sb("be2s", [E, D], BF16, c7)
        b_be2 = nb("be2s")
        p.dma("gpsimd", be2s[:], be2, writes=[b_be2])
        sel_sb = sb("sel_sb", [E, E * 128], F32, c7)
        p.dma("sync", sel_sb[:], sel_h.ap(), writes=[b_const])
        lg5 = sb("lg5", [4, TB], F32, c7); b_lg = nb("lg5")
        gsc = sb("gsc", [128, 48], F32, c7); b_gsc = nb("gsc")
        ctm5 = sb("ctm5", [128, 4], F32, c7); b_ctm = nb("ctm5")
        comb_fm = sb("comb_fm", [4, TB], F32, c7); b_cfm = nb("cfm")
        combb = sb("combb", [4, TB], BF16, c7); b_cbb = nb("cbb")
        cbc = sb("cbc", [128, E * TB], F32, c7); b_cbc = Buf("cbc", noww=True)
        gel = sb("gel", [128, TB], F32, c7); b_gel = nb("gel")
        osb = sb("osb", [128, TB], F32, c7); b_osb = nb("osb")
        xc7 = sb("xc7", [128, 2 * TB], F32, c7)
        b_xc7 = [nb("x7a"), nb("x7b")]
        hc7 = sb("hc7", [128, 2 * TB], F32, c7)
        b_hc7 = [nb("h7a"), nb("h7b")]
        for s_ in range(OWNT):
            tcols = slice(s_ * TB, (s_ + 1) * TB)
            for fc in range(FC):
                p.dma("sync", g2s[:, fc * TB:(fc + 1) * TB], g2p[fc, :, tcols],
                      reads=[DB["g2p"]], writes=[b_g2s[fc]])
            for fc in range(FC):
                p.op("tensor", lambda e, l=wg_sb[:, fc * E:(fc + 1) * E],
                     r=g2s[:, fc * TB:(fc + 1) * TB], s=(fc == 0), st=(fc == FC - 1):
                     e.matmul(pC[0:E, :], l, r, start=s, stop=st),
                     reads=[b_g2s[fc], b_const], writes=[b_pC])
            p.op("vector", lambda e: e.tensor_copy(lg5[:], pC[0:E, :]),
                 reads=[b_pC], writes=[b_lg])
            for cc_ in range(4):
                p.op("tensor", lambda e, ii=lg5[0:4, cc_ * 128:(cc_ + 1) * 128]:
                     e.transpose(pTf[0:128, 0:4], ii, eyef_sb[0:4, 0:4]),
                     reads=[b_lg, b_const], writes=[b_pTf])
                G = lambda j: gsc[:, 4 * j:4 * (j + 1)]
                C1 = lambda j: gsc[:, 32 + j:33 + j]
                p.op("vector", lambda e: e.tensor_copy(G(0), pTf[0:128, 0:4]),
                     reads=[b_pTf], writes=[b_gsc])
                p.op("vector", lambda e: e.tensor_reduce(C1(0), G(0), AX.X, ALU.max, negate=True),
                     reads=[b_gsc], writes=[b_gsc])
                p.op("scalar", lambda e: e.activation(G(1), G(0), AF.Exp, bias=C1(0), scale=1.0),
                     reads=[b_gsc], writes=[b_gsc])
                p.op("vector", lambda e: e.tensor_reduce(C1(1), G(1), AX.X, ALU.add),
                     reads=[b_gsc], writes=[b_gsc])
                p.op("vector", lambda e: e.reciprocal(C1(2), C1(1)),
                     reads=[b_gsc], writes=[b_gsc])
                p.op("vector", lambda e: e.tensor_scalar(G(2), G(1), C1(2), None, ALU.mult),
                     reads=[b_gsc], writes=[b_gsc])
                p.op("vector", lambda e: e.tensor_reduce(C1(3), G(2), AX.X, ALU.max),
                     reads=[b_gsc], writes=[b_gsc])
                p.op("vector", lambda e: e.tensor_scalar(G(3), G(2), C1(3), None, ALU.is_ge),
                     reads=[b_gsc], writes=[b_gsc])
                p.op("vector", lambda e: e.scalar_tensor_tensor(G(4), G(3), NEGB, G(2), ALU.mult, ALU.add),
                     reads=[b_gsc], writes=[b_gsc])
                p.op("vector", lambda e: e.tensor_reduce(C1(4), G(4), AX.X, ALU.max),
                     reads=[b_gsc], writes=[b_gsc])
                p.op("vector", lambda e: e.tensor_scalar(G(5), G(2), C1(4), None, ALU.is_ge),
                     reads=[b_gsc], writes=[b_gsc])
                p.op("scalar", lambda e: e.activation(G(6), G(2), AF.Exp, bias=0.0, scale=1.0),
                     reads=[b_gsc], writes=[b_gsc])
                p.op("vector", lambda e: e.tensor_tensor(G(6), G(6), G(5), ALU.mult),
                     reads=[b_gsc], writes=[b_gsc])
                p.op("vector", lambda e: e.tensor_reduce(C1(5), G(6), AX.X, ALU.add),
                     reads=[b_gsc], writes=[b_gsc])
                p.op("vector", lambda e: e.reciprocal(C1(6), C1(5)),
                     reads=[b_gsc], writes=[b_gsc])
                p.op("vector", lambda e: e.tensor_scalar(ctm5[:], G(6), C1(6), None, ALU.mult),
                     reads=[b_gsc], writes=[b_ctm])
                if "gsc" in debug and cc_ == 0:
                    if s_ == 0:
                        emit_p7.dbg_gsc = nc.dram_tensor(
                            "dbg_gsc", [OWNT * 128, 48], F32, kind="ExternalOutput").ap()
                    p.dma("sync", emit_p7.dbg_gsc[s_ * 128:(s_ + 1) * 128, :], gsc[:],
                          reads=[b_gsc], writes=[Buf("dbggsc")])
                p.op("tensor", lambda e: e.transpose(pTf[0:4, 0:128], ctm5[:], eyef_sb[:]),
                     reads=[b_ctm, b_const], writes=[b_pTf])
                p.op("vector", lambda e, o=comb_fm[0:4, cc_ * 128:(cc_ + 1) * 128]:
                     e.tensor_copy(o, pTf[0:4, 0:128]),
                     reads=[b_pTf], writes=[b_cfm])
            if "comb" in debug:
                if s_ == 0:
                    emit_p7.dbg_comb = nc.dram_tensor(
                        "dbg_comb", [4 * OWNT, TB], F32, kind="ExternalOutput").ap()
                p.dma("sync", emit_p7.dbg_comb[s_ * 4:(s_ + 1) * 4, :], comb_fm[:],
                      reads=[b_cfm], writes=[Buf("dbgcomb")])
            p.op("vector", lambda e: e.tensor_copy(combb[:], comb_fm[:]),
                 reads=[b_cfm], writes=[b_cbb])
            for e_ in range(E):
                p.op("tensor", lambda e, l=sel_sb[0:4, e_ * 128:(e_ + 1) * 128]:
                     e.matmul(pC[:], l, comb_fm[:], start=True, stop=True),
                     reads=[b_cfm, b_const], writes=[b_pC])
                p.op("vector", lambda e, o=cbc[:, e_ * TB:(e_ + 1) * TB]:
                     e.tensor_copy(o, pC[:]), reads=[b_pC], writes=[b_cbc])
            for e_ in range(E):
                for ic in range(IC):
                    wsl = (e_ * IC + ic) % 2
                    p.dma("gpsimd", w1sl[:, wsl * D:(wsl + 1) * D], we1[e_, ic],
                          writes=[b_w1sl[wsl]])
                    for fc in range(FC):
                        p.op("tensor", lambda e,
                             l=w1sl[:, wsl * D + fc * 128: wsl * D + (fc + 1) * 128],
                             r=g2s[:, fc * TB:(fc + 1) * TB],
                             s=(fc == 0), st=(fc == FC - 1):
                             e.matmul(pA[:], l, r, start=s, stop=st),
                             reads=[b_g2s[fc], b_w1sl[wsl]], writes=[b_pA])
                        p.op("scalar", lambda e, bi=be1_sb[:, e_ * IC + ic: e_ * IC + ic + 1]:
                             e.activation(gel[:], pA[:], AF.Gelu, bias=bi, scale=1.0),
                             reads=[b_pA, b_const], writes=[b_gel]) if fc == FC - 1 else None
                    p.op("vector", lambda e,
                         o=hmid[:, (e_ * IC + ic) * TB:(e_ * IC + ic + 1) * TB],
                         cb=cbc[:, e_ * TB:(e_ + 1) * TB]:
                         e.tensor_tensor(o, gel[:], cb, ALU.mult),
                         reads=[b_gel, b_cbc], writes=[b_hmid])
            for oc in range(FC):
                for e_ in range(E):
                    j = (oc % 2) * 4 + e_
                    p.dma("gpsimd", w2sl[:, j * D:(j + 1) * D], we2[e_, oc],
                          writes=[b_w2sl[j]])
                for e_ in range(E):
                    j = (oc % 2) * 4 + e_
                    for ic in range(IC):
                        p.op("tensor", lambda e,
                             l=w2sl[:, j * D + ic * 128: j * D + (ic + 1) * 128],
                             r=hmid[:, (e_ * IC + ic) * TB:(e_ * IC + ic + 1) * TB],
                             s=(e_ == 0 and ic == 0):
                             e.matmul(pB[:], l, r, start=s, stop=False),
                             reads=[b_hmid, b_w2sl[j]], writes=[b_pB])
                p.op("tensor", lambda e, l=be2s[0:4, oc * 128:(oc + 1) * 128]:
                     e.matmul(pB[:], l, combb[:], start=False, stop=True),
                     reads=[b_be2, b_cbb], writes=[b_pB])
                xsl = oc % 2
                xv = xc7[:, xsl * TB:(xsl + 1) * TB]
                hv = hc7[:, xsl * TB:(xsl + 1) * TB]
                p.dma("sync", xv, x1p[oc, :, tcols], reads=[DB["x1p"]], writes=[b_xc7[xsl]])
                p.dma("sync", hv, hp[oc, :, tcols], reads=[DB["hp"]], writes=[b_hc7[xsl]])
                p.op("vector", lambda e, x_=xv: e.tensor_tensor(osb[:], pB[:], x_, ALU.add),
                     reads=[b_pB, b_xc7[xsl]], writes=[b_osb])
                p.op("vector", lambda e, h_=hv: e.tensor_tensor(osb[:], osb[:], h_, ALU.add),
                     reads=[b_hc7[xsl]], writes=[b_osb])
                p.dma("sync", OUT[oc * 128:(oc + 1) * 128, tcols], osb[:],
                      reads=[b_osb], writes=[DB["out"]])
        c7.close()

    if phase_limit >= 2:
        ctx3, g1_sb, b_g1 = emit_p23()
        if phase_limit >= 3:
            emit_p3b(g1_sb, b_g1)
        ctx3.close()
        p.barrier()
        if phase_limit >= 4:
            p4 = emit_p4()
            c4, y2_acc, b_y2acc = p4
            if phase_limit >= 5:
                emit_p56(y2_acc, b_y2acc)
            c4.close()
            p.barrier()
            if phase_limit >= 7:
                emit_p7()
    if phase_limit < 7:
        p.dma("sync", OUT[0:128, 0:len(CONSTS)], cp_sb[:], reads=[b_const],
              writes=[DB["out"]])

    dbg_map = {"q1p": (q1p, 1), "k1p": (k1p, 1), "vnp": (vnp, 1), "v1tp": (v1tp, 0),
               "x1p": (x1p, 1), "q2p": (q2p, 1), "k2g": (k2g, 0), "v2g": (v2g, 0),
               "hp": (hp, 1), "g2p": (g2p, 1), "rows_d": (rows_d, 0)}
    if "nodbg" in _KSKIP:
        debug = ()
    for nm in debug:
        if nm not in dbg_map:
            continue
        src, flatten = dbg_map[nm]
        flat = src.rearrange("a b c -> (a b) c") if flatten else src
        o = nc.dram_tensor("dbg_" + nm, [flat.shape[0], flat.shape[1]],
                           src.tensor.dtype, kind="ExternalOutput").ap()
        p.dma("sync", o, flat, reads=[DB[nm]], writes=[Buf("dbg")])
    if "nobar2" not in _KSKIP:
        p.barrier()

    if "nomark" not in _KSKIP:
        w = p._mkop("sync", None, "y", 0)
        w.waits = [("ABS", p.sems[f"q{j}"], p.counts[f"q{j}"] * 16)
                   for j in range(QPOOL) if p.counts[f"q{j}"]]
        w.waits += [("ABS", p.sems[f"g{j}"], p.counts[f"g{j}"] * 16)
                    for j in range(3) if p.counts[f"g{j}"]]
        if p.counts["cc"]:
            w.waits.append(("ABS", p.sems["cc"], p.counts["cc"]))

    p.finish()
    ctx.close()
    return nc


def _build_packs(I):
    pk = {}
    pk["wq1"] = _pack_col(np.asarray(I["L0_attn_Wq"]), H1, FC)
    pk["wk1"] = _pack_col(np.asarray(I["L0_attn_Wk"]), KVH1, FC)
    pk["wv1"] = _pack_col(np.asarray(I["L0_attn_Wv"]), KVH1, FC)
    pk["wpj"] = _pack_col(np.asarray(I["L0_attn_Wproj"]), FC, FC)
    pk["wq2"] = _pack_col(np.asarray(I["L0_ffn_wq"]), FC, FC)
    pk["wk2"] = _pack_col(np.asarray(I["L0_ffn_wk"]), FC, FC)
    pk["wv2"] = _pack_col(np.asarray(I["L0_ffn_wv"]), FC, FC)
    pk["wo2"] = _pack_col(np.asarray(I["L0_ffn_wo"]), FC, FC)
    pk["qa"] = _pack_A(np.asarray(I["L0_attn_q_A"])); pk["qb"] = np.asarray(I["L0_attn_q_B"])
    pk["ka"] = _pack_A(np.asarray(I["L0_attn_k_A"])); pk["kb"] = np.asarray(I["L0_attn_k_B"])
    pk["va"] = _pack_A(np.asarray(I["L0_attn_v_A"])); pk["vb"] = np.asarray(I["L0_attn_v_B"])
    pk["pa"] = _pack_A(np.asarray(I["L0_attn_p_A"])); pk["pb"] = np.asarray(I["L0_attn_p_B"])
    pk["gain"] = np.ascontiguousarray(np.asarray(I["L0_attn_q_gain"]).reshape(1, H1))
    for nm, key in (("bq2", "L0_ffn_bq"), ("bk2", "L0_ffn_bk"), ("bv2", "L0_ffn_bv"),
                    ("bo2", "L0_ffn_bo"), ("l1w", "L0_ffn_ln1_w"), ("l1b", "L0_ffn_ln1_b"),
                    ("l2w", "L0_ffn_ln2_w"), ("l2b", "L0_ffn_ln2_b")):
        pk[nm] = _pack_vec(np.asarray(I[key]))
    pk["wg"] = np.ascontiguousarray(
        np.asarray(I["L0_ffn_w_gate"]).reshape(FC, 128, E).transpose(1, 0, 2).reshape(128, FC * E))
    we1 = np.asarray(I["L0_ffn_we1"])
    pk["we1"] = np.ascontiguousarray(
        we1.reshape(E, FC, 128, IC, 128).transpose(0, 3, 2, 1, 4).reshape(E, IC, 128, D))
    pk["be1"] = np.ascontiguousarray(
        np.asarray(I["L0_ffn_be1"]).reshape(E, IC, 128).transpose(2, 0, 1).reshape(128, E * IC))
    we2 = np.asarray(I["L0_ffn_we2"])
    pk["we2"] = np.ascontiguousarray(
        we2.reshape(E, IC, 128, FC, 128).transpose(0, 3, 2, 1, 4).reshape(E, FC, 128, I_DIM))
    pk["be2"] = np.asarray(I["L0_ffn_be2"])
    return pk


def _stage_core(I, pk, b, r):
    own = SNAKE[r]
    sib = SNAKE[1 - r]
    loc = own + sib
    tok = np.concatenate([np.arange(g * TB, (g + 1) * TB) for g in loc])
    d = dict(pk)
    x = np.asarray(I["L0_attn_x"][b])
    d["xfm"] = np.ascontiguousarray(x[tok].T)

    pos = tok.astype(np.float64)
    inv1 = 10000.0 ** (-np.arange(0, ROPE1, 2, dtype=np.float64) / ROPE1)
    fr1 = inv1[:, None] * pos[None, :]
    d["cos1"] = np.cos(fr1).astype(np.float32)
    d["sin1"] = np.sin(fr1).astype(np.float32)
    pos2 = pos[:TOWN]
    inv2 = 10000.0 ** (-np.arange(0, HD, 2, dtype=np.float64) / HD)
    fr2 = inv2[:, None] * pos2[None, :]
    d["cos2"] = np.cos(fr2).astype(np.float32)
    d["sin2"] = np.sin(fr2).astype(np.float32)

    bB1 = np.array([(-C0A if sib[i] < own[i] else NEGB) for i in range(4)], np.float32)
    d["biasB1"] = np.ascontiguousarray(np.tile(bB1[None, :], (128, 1)))
    ev, od = SNAKE[0], SNAKE[1]
    if r == 0:
        bA2 = np.full(4, -C0B, np.float32)
        bB2 = np.array([(-C0B if od[i] < ev[i] else NEGB) for i in range(4)], np.float32)
        mat = np.asarray(_tri_np(), dtype=np.float32)
        mbt = np.ones((128, 4 * TB), np.float32)
    else:
        bA2 = np.array([(-C0B if ev[i] < od[i] else NEGB) for i in range(4)], np.float32)
        bB2 = np.full(4, -C0B, np.float32)
        mat = np.ones((128, 4 * TB), np.float32)
        mbt = np.asarray(_tri_np(), dtype=np.float32)
    d["biasA2"] = np.ascontiguousarray(np.tile(bA2[None, :], (128, 1)))
    d["biasB2"] = np.ascontiguousarray(np.tile(bB2[None, :], (128, 1)))
    d["mat_in"] = np.ascontiguousarray(mat.astype(BB))
    d["mbt_in"] = np.ascontiguousarray(mbt.astype(BB))
    return d, tok


def _run(inputs, debug=(), phase_limit=None):
    pk = _build_packs(inputs)
    nc = build_program(phase_limit=phase_limit, debug=debug)
    in_maps = []
    toks = []
    for c in range(8):
        d, tok = _stage_core(inputs, pk, c // 2, c % 2)
        in_maps.append(d)
        toks.append(tok)
    res = run_bass_kernel_spmd(nc, in_maps, core_ids=list(range(8)))
    return res, toks


def kernel(**inputs):
    res, toks = _run(inputs, debug=DEBUG_DUMPS)
    out = np.empty((B, T, D), np.float32)
    for c in range(8):
        tok_own = toks[c][:TOWN]
        out[c // 2, tok_own, :] = np.asarray(res.results[c]["out"]).T
    return out
